# version 17
# baseline (speedup 1.0000x reference)
import numpy as np
import concourse.bass as bass
import concourse.mybir as mybir
from concourse.bass_utils import run_bass_kernel_spmd

F32 = mybir.dt.float32
BF16 = mybir.dt.bfloat16
AF = mybir.ActivationFunctionType
ALU = mybir.AluOpType
AX = mybir.AxisListType

L = 4096
DM = 1024
NHC = 8
HD = 64
DI = 4
CH = 16
NCH = L // CH
PT = 512
NPT = L // PT
SW = 32
OW = 16
TOKH = L // 2
FFB = 512
NEG = -1.0e30
ENGINES = ("sync", "tensor", "vector", "scalar", "gpsimd")



def host_prep(inputs):
    hid = np.asarray(inputs["L0_attn_hidden_states"], np.float32)
    Wq = np.asarray(inputs["L0_attn_Wq"], np.float32)
    Wk = np.asarray(inputs["L0_attn_Wk"], np.float32)
    Wv = np.asarray(inputs["L0_attn_Wv"], np.float32)
    Wlr = np.asarray(inputs["L0_attn_Wlr"], np.float32)
    Wg = np.asarray(inputs["L0_attn_Wg"], np.float32)
    Wo = np.asarray(inputs["L0_attn_Wo"], np.float32)
    cq = np.asarray(inputs["L0_attn_cq"], np.float32)
    ck = np.asarray(inputs["L0_attn_ck"], np.float32)
    cv = np.asarray(inputs["L0_attn_cv"], np.float32)
    Wi0 = np.asarray(inputs["L0_attn_W_in_init"], np.float32)
    Wo0 = np.asarray(inputs["L0_attn_W_out_init"], np.float32)
    ln_g = np.asarray(inputs["L0_attn_ln_g"], np.float32)
    ln_b = np.asarray(inputs["L0_attn_ln_b"], np.float32)

    ident = np.eye(128, dtype=np.float32)
    halfsel = np.zeros((128, 2), np.float32)
    halfsel[:64, 0] = 1.0
    halfsel[64:, 1] = 1.0
    halfbc = np.zeros((2, 128), np.float32)
    halfbc[0, :64] = 1.0
    halfbc[1, 64:] = 1.0
    ones_col = np.ones((128, 1), np.float32)
    ones_row = np.ones((1, 128), np.float32)
    lrsel = np.zeros((32, 4 * 128), np.float32)
    for f in range(4):
        for pf in range(128):
            lrsel[4 * (2 * f + pf // 64), 128 * f + pf] = 1.0
    bd16 = np.zeros((128, 16), np.float32)
    for p in range(128):
        bd16[p, p // 16] = 1.0
        bd16[p, 8 + p // 16] = 1.0
    smask = np.full((64, 128), NEG, np.float32)
    for p in range(64):
        lane = (p % 32) // 4
        smask[p, lane * 16:(lane + 1) * 16] = 0.0
    smk_l = np.zeros((9, 64), np.float32)
    smk_r = np.zeros((9, 128), np.float32)
    smk_l[0, :] = -1.0e30
    smk_r[0, :] = 1.0
    for l in range(8):
        for p in range(64):
            if (p % 32) // 4 == l:
                smk_l[1 + l, p] = 1.0e30
        smk_r[1 + l, l * 16:(l + 1) * 16] = 1.0
    mtril = np.zeros((128, 128), np.float32)
    for tk in range(128):
        for tq in range(128):
            if tk // 16 == tq // 16 and (tq % 16) >= (tk % 16):
                mtril[tk, tq] = 1.0

    per_core = []
    for c in range(8):
        b, hh = c // 2, c % 2
        h0 = hh * NHC
        sl = slice(h0 * HD, h0 * HD + NHC * HD)
        m = {}
        m["hid_T"] = np.ascontiguousarray(hid[b].T)
        m["hid_half"] = np.ascontiguousarray(hid[b].T[:, hh * TOKH:(hh + 1) * TOKH])
        m["wq_T"] = np.ascontiguousarray(Wq[sl, :].T)
        m["wk_T"] = np.ascontiguousarray(Wk[sl, :].T)
        m["wv_T"] = np.ascontiguousarray(Wv[sl, :].T)
        m["wg_T"] = np.ascontiguousarray(Wg[sl, :].T)
        m["wo_T"] = np.ascontiguousarray(Wo.T)
        wls = np.zeros((DM, 64), np.float32)
        for p in range(32):
            wls[:, p] = Wlr[2 * (h0 + p // 4) + 1, :]
            wls[:, 32 + p] = Wlr[2 * (h0 + p // 4), :]
        m["wlr_scan"] = wls
        m["cq_s"] = np.ascontiguousarray(cq[sl, :])
        m["ck_s"] = np.ascontiguousarray(ck[sl, :])
        m["cv_s"] = np.ascontiguousarray(cv[sl, :])
        wp = np.zeros((HD, 64), np.float32)
        for lane in range(NHC):
            for D in range(DI):
                wp[:, lane * 4 + D] = Wi0[0, D, h0 + lane, :]
                wp[:, 32 + lane * 4 + D] = Wo0[0, D, h0 + lane, :]
        m["wp_init"] = wp
        m["ln_g"] = ln_g.reshape(HD, 1).copy()
        m["ln_b"] = ln_b.reshape(HD, 1).copy()
        m["w1_T"] = np.ascontiguousarray(np.asarray(inputs["L0_ffn_w1_w"], np.float32).T)
        m["w2_T"] = np.ascontiguousarray(np.asarray(inputs["L0_ffn_w2_w"], np.float32).T)
        m["h1_T"] = np.ascontiguousarray(np.asarray(inputs["L0_ffn_h1_w"], np.float32).T)
        m["h2_T"] = np.ascontiguousarray(np.asarray(inputs["L0_ffn_h2_w"], np.float32).T)
        m["w3_T"] = np.ascontiguousarray(np.asarray(inputs["L0_ffn_w3_w"], np.float32).T)
        m["s1"] = np.ascontiguousarray(np.asarray(inputs["L0_ffn_w1_s"], np.float32).reshape(32, 128).T)
        m["s2"] = np.ascontiguousarray(np.asarray(inputs["L0_ffn_w2_s"], np.float32).reshape(32, 128).T)
        m["sh1"] = np.ascontiguousarray(np.asarray(inputs["L0_ffn_h1_s"], np.float32).reshape(32, 128).T)
        m["sh2"] = np.ascontiguousarray(np.asarray(inputs["L0_ffn_h2_s"], np.float32).reshape(32, 128).T)
        m["s3"] = np.ascontiguousarray(np.asarray(inputs["L0_ffn_w3_s"], np.float32).reshape(8, 128).T)
        m.update(ident=ident, halfsel=halfsel, halfbc=halfbc, ones_col=ones_col,
                 ones_row=ones_row, lrsel=lrsel, bd16=bd16, smask=smask,
                 mtril=mtril, smk_l=smk_l, smk_r=smk_r)
        per_core.append(m)
    return per_core



class Streams:
    def __init__(self):
        self.ops = {e: [] for e in ENGINES}


class KB:
    def __init__(self, debug_taps=()):
        self.nc = bass.Bass("TRN2", target_bir_lowering=False)
        self.debug_taps = set(debug_taps)
        self.taps = {}
        self.cnt = {}
        self.st = None
        self._es = None
        self._cur_es = None
        self.t = {}

    def emit(self, eng, meth, *args, waits=(), drain=False, inc=None,
             incv=1, **kw):
        def f(E, S, args=args, kw=kw, waits=tuple(waits)):
            for s, v in waits:
                E.wait_ge(S[s], v)
            if drain:
                E.drain()
            ins = getattr(E, meth)(*args, **kw)
            if inc is not None:
                ins.then_inc(S[inc], incv)
        self.st.ops[eng].append(f)
        if inc is not None:
            self.cnt[inc] += incv
            return self.cnt[inc]
        return None

    def wait(self, eng, sem, val):
        def f(E, S):
            E.wait_ge(S[sem], val)
        self.st.ops[eng].append(f)

    def raw(self, eng, fn):
        self.st.ops[eng].append(fn)

    def inc_here(self, eng, sem, incv=1):
        self.cnt[sem] += incv
        v = self.cnt[sem]
        def f(E, S):
            E.nop().then_inc(S[sem], incv)
        self.st.ops[eng].append(f)
        return v

    def barrier(self):
        tgt = self.cnt["bar"] + 5
        self.cnt["bar"] = tgt
        for e in ENGINES:
            def f(E, S, t=tgt):
                E.nop().then_inc(S["bar"], 1)
                E.wait_ge(S["bar"], t)
            self.st.ops[e].append(f)

    def flush(self, label=None):
        import contextlib
        cm = self.nc.named_scope(label) if label else contextlib.nullcontext()
        with cm:
            for e in ENGINES:
                ops = self.st.ops[e]
                if not ops:
                    continue
                def body(E, ops=ops, S=self.sems):
                    for f in ops:
                        f(E, S)
                getattr(self.block, e)(body)
        self.st = Streams()

    def build(self):
        from contextlib import ExitStack
        nc = self.nc
        self.din = {}
        def din(name, shape, dt=F32):
            self.din[name] = nc.dram_tensor(name, list(shape), dt,
                                            kind="ExternalInput").ap()
        din("hid_T", (DM, L))
        din("hid_half", (DM, TOKH))
        for w in ("wq_T", "wk_T", "wv_T", "wg_T"):
            din(w, (DM, 512))
        din("wo_T", (DM, DM))
        din("wlr_scan", (DM, 64))
        for w in ("cq_s", "ck_s", "cv_s"):
            din(w, (512, 4))
        din("wp_init", (HD, 64))
        din("ln_g", (HD, 1)); din("ln_b", (HD, 1))
        din("w1_T", (DM, 4096)); din("w2_T", (DM, 4096))
        din("h1_T", (4096, 4096)); din("h2_T", (4096, 4096))
        din("w3_T", (4096, DM))
        din("s1", (128, 32)); din("s2", (128, 32)); din("sh1", (128, 32))
        din("sh2", (128, 32)); din("s3", (128, 8))
        din("ident", (128, 128))
        din("halfsel", (128, 2)); din("halfbc", (2, 128))
        din("ones_col", (128, 1)); din("ones_row", (1, 128))
        din("lrsel", (32, 512))
        din("bd16", (128, 16)); din("smask", (64, 128)); din("mtril", (128, 128))
        din("smk_l", (9, 64)); din("smk_r", (9, 128))

        self.out = nc.dram_tensor("out_T", [DM, TOKH], F32, kind="ExternalOutput").ap()

        self.sc_feedA = nc.dram_tensor("sc_feedA", [128, NCH * 128], BF16, kind="Internal")
        self.sc_feedB = nc.dram_tensor("sc_feedB", [128, NCH * 128], BF16, kind="Internal")
        self.sc_gate = nc.dram_tensor("sc_gate", [512, L], BF16, kind="Internal")
        self.sc_o = nc.dram_tensor("sc_o", [HD, NCH * 128], BF16, kind="Internal")
        self.sc_og = nc.dram_tensor("sc_og", [512, L], BF16, kind="Internal")
        self.sc_w1 = nc.dram_tensor("sc_w1", [DM, 4096], BF16, kind="Internal")
        self.sc_w2 = nc.dram_tensor("sc_w2", [DM, 4096], BF16, kind="Internal")
        self.sc_h1 = nc.dram_tensor("sc_h1", [4096, 4096], BF16, kind="Internal")
        self.sc_h2 = nc.dram_tensor("sc_h2", [4096, 4096], BF16, kind="Internal")
        self.sc_w3 = nc.dram_tensor("sc_w3", [4096, DM], BF16, kind="Internal")
        self.og_full = nc.dram_tensor("og_full", [DM, L], BF16, kind="Internal")

        tapspec = {
            "feedA": ([128, NCH * 128], BF16), "feedB": ([128, NCH * 128], BF16),
            "gate": ([512, L], BF16), "lrC": ([HD, L], F32),
            "o": ([HD, NCH * 128], BF16), "og": ([DM, L], BF16),
            "xmid": ([128, 8 * 512], F32),
        }
        for name in self.debug_taps:
            shp, dt = tapspec[name]
            self.taps[name] = nc.dram_tensor("tap_" + name, shp, dt,
                                             kind="ExternalOutput").ap()

        es = ExitStack()
        self._es = es
        with es:
            self.sems = {}
            for s in ("sy", "syB", "gd", "gdB", "gdO", "of", "ofB",
                      "wg0", "wg1", "wg2", "wg3", "wh0", "wh1", "wh2", "wh3",
                      "wcL", "wcS",
                      "pe", "dv", "ac", "gp", "cc", "bar", "so"):
                self.sems[s] = es.enter_context(nc.semaphore("sem_" + s))
                self.cnt[s] = 0
            self.block = es.enter_context(nc.Block())
            self.st = Streams()

            self.pb = [es.enter_context(nc.psum_tensor(f"pb{i}", [128, 512], F32))
                       for i in range(6)]
            self.pbf = [es.enter_context(nc.psum_tensor(f"pbf{i}", [128, 1024], BF16))
                        for i in range(2)]

            self._alloc_resident(es)
            self._p0()
            self.flush('p0')
            self._p1(es)
            self._p2(es)
            self._p3(es)
            self._p4(es)
        return nc

    def SB(self, name, shape, dt):
        es = self._cur_es if self._cur_es is not None else self._es
        h = es.enter_context(self.nc.sbuf_tensor(name, list(shape), dt))
        self.t[name] = h
        return h

    def phase_scope(self):
        from contextlib import ExitStack
        self._cur_es = ExitStack()
        return self._cur_es

    def end_scope(self):
        self._cur_es.close()
        self._cur_es = None

    def _alloc_resident(self, es):
        SB = self.SB
        SB("convq_sb", (128, 4 * 5), F32)
        SB("convk_sb", (128, 4 * 5), F32)
        SB("convv_sb", (128, 4 * 5), F32)
        SB("wp_sb", (HD, 64), F32)
        SB("wpbf_sb", (HD, 64), BF16)
        SB("lng_sb", (HD, 1), F32)
        SB("lnb_sb", (HD, 1), F32)
        SB("ident_sb", (128, 128), F32)
        SB("identb_sb", (128, 128), BF16)
        SB("halfsel_sb", (128, 2), F32)
        SB("halfbc_sb", (2, 128), F32)
        SB("onesc_sb", (128, 1), F32)
        SB("onesr_sb", (1, 128), F32)
        SB("onescb_sb", (128, 1), BF16)
        SB("onesrb_sb", (1, 128), BF16)
        SB("lrsel_sb", (32, 512), F32)
        SB("bd16_sb", (128, 16), F32)
        SB("smask_sb", (64, 128), F32)
        SB("smkl_sb", (9, 64), BF16)
        SB("smkr_sb", (9, 128), BF16)
        SB("mtril_sb", (128, 128), F32)
        SB("cb_sb", (128, 4), F32)
        SB("s1_sb", (128, 32), F32)
        SB("s2_sb", (128, 32), F32)
        SB("sh1_sb", (128, 32), F32)
        SB("sh2_sb", (128, 32), F32)
        SB("s3_sb", (128, 8), F32)

    def _p0(self):
        t, d = self.t, self.din
        def loads(E, S):
            for name, src in (("ident_sb", "ident"), ("halfsel_sb", "halfsel"),
                              ("halfbc_sb", "halfbc"), ("onesc_sb", "ones_col"),
                              ("onesr_sb", "ones_row"), ("lrsel_sb", "lrsel"),
                              ("bd16_sb", "bd16"), ("smask_sb", "smask"),
                              ("mtril_sb", "mtril"), ("wp_sb", "wp_init"),
                              ("lng_sb", "ln_g"), ("lnb_sb", "ln_b"),
                              ("s1_sb", "s1"), ("s2_sb", "s2"),
                              ("sh1_sb", "sh1"), ("sh2_sb", "sh2"),
                              ("s3_sb", "s3")):
                E.dma_start(out=t[name][:], in_=d[src]).then_inc(S["sy"], 16)
        self.st.ops["sync"].append(loads)
        self.cnt["sy"] += 16 * 17
        v_sy = self.cnt["sy"]

        def casts(E, S):
            for name, src in (("convq_sb", "cq_s"), ("convk_sb", "ck_s"),
                              ("convv_sb", "cv_s")):
                E.dma_start(
                    out=t[name][:].rearrange("p (f k) -> p f k", k=5)[:, :, 0:4],
                    in_=d[src].rearrange("(f p) k -> p f k", p=128)
                    ).then_inc(S["gd"], 16)
            E.dma_start(out=t["smkl_sb"][:], in_=d["smk_l"]).then_inc(S["gd"], 16)
            E.dma_start(out=t["smkr_sb"][:], in_=d["smk_r"]).then_inc(S["gd"], 16)
            E.memset(t["cb_sb"][:, 0:1], 1e-5)
            E.memset(t["cb_sb"][:, 1:2], 0.001)
            E.memset(t["cb_sb"][:, 2:3], 1.0)
            ins2 = E.memset(t["cb_sb"][:, 3:4], 0.0)
            ins2.then_inc(S["gp"], 1)
        self.st.ops["gpsimd"].append(casts)
        self.cnt["gp"] += 1
        self.cnt["gd"] += 16 * 5
        v_gd = self.cnt["gd"]

        def vsetup(E, S):
            E.wait_ge(S["sy"], v_sy)
            E.wait_ge(S["gd"], v_gd)
            for name in ("convq_sb", "convk_sb", "convv_sb"):
                cw = t[name]
                for f in range(4):
                    E.tensor_scalar_add(cw[:, f * 5 + 4:f * 5 + 5],
                                        cw[:, f * 5 + 3:f * 5 + 4], 1.0)
            E.tensor_copy(t["identb_sb"][:], t["ident_sb"][:])
            E.tensor_copy(t["onesrb_sb"][:], t["onesr_sb"][:])
            E.tensor_copy(t["onescb_sb"][:], t["onesc_sb"][:])
            ins = E.tensor_copy(t["wpbf_sb"][:], t["wp_sb"][:])
            ins.then_inc(S["dv"], 1)
        self.st.ops["vector"].append(vsetup)
        self.cnt["dv"] += 1
        self.barrier()

    def _p1(self, es):
        t, d, pb = self.t, self.din, self.pb
        from contextlib import ExitStack
        self._es_mixer = ExitStack()
        self._cur_es = self._es_mixer
        SB = self.SB
        SB("lrC_sb", (HD, L), F32)
        self.phase_scope()
        SB("wq_sb", (128, 8 * 512), BF16)
        SB("wk_sb", (128, 8 * 512), BF16)
        SB("wv_sb", (128, 8 * 512), BF16)
        SB("wg_sb", (128, 8 * 512), BF16)
        SB("wls_sb", (128, 8 * 64), BF16)
        em0, wt0 = self.emit, self.wait
        def wcasts(E, S):
            for name, src in (("wq_sb", "wq_T"), ("wk_sb", "wk_T"),
                              ("wv_sb", "wv_T"), ("wg_sb", "wg_T")):
                E.dma_start(out=t[name][:].rearrange("p (a f) -> p a f", a=8),
                            in_=d[src].rearrange("(a p) f -> p a f", p=128)
                            ).then_inc(S["gd"], 16)
            E.dma_start(out=t["wls_sb"][:].rearrange("p (a f) -> p a f", a=8),
                        in_=d["wlr_scan"].rearrange("(a p) f -> p a f", p=128)
                        ).then_inc(S["gd"], 16)
        self.st.ops["gpsimd"].append(wcasts)
        self.cnt["gd"] += 16 * 5
        v_wc = self.cnt["gd"]
        wt0("tensor", "gd", v_wc)
        hidt = SB("p1_hidt", (128, 8 * PT), F32)
        sqt = SB("p1_sqt", (128, 8 * PT), F32)
        ln1t = SB("p1_ln1t", (128, 8 * PT), BF16)
        stat = SB("p1_stat", (1, 4 * PT), F32)
        convh = SB("p1_convh", (128, 2 * (PT + 3)), F32)
        cvt = SB("p1_cvt", (128, 2 * PT), F32)
        tails = SB("p1_tails", (128, 12 * 3), F32)
        silt = SB("p1_silt", (128, 2 * PT), F32)
        sq2t = SB("p1_sq2t", (128, 2 * PT), F32)
        rnorm = SB("p1_rnorm", (2, 4 * PT), F32)
        fint = SB("p1_fint", (128, 2 * PT), F32)

        def _p1_feed(g, kind, f, src):
            dstt, roff = {"k": (asmA, 0), "v": (asmA, 64),
                          "vl": (asmB, 0), "q": (asmB, 64)}[kind]
            last = None
            for hp in range(2):
                lane = 2 * f + hp
                last = self.emit("vector", "tensor_copy",
                                 dstt[roff:roff + 64, :]
                                 .rearrange("p (c l u) -> p c l u", l=8, u=16)
                                 [:, :, lane, :],
                                 src[hp * 64:(hp + 1) * 64, :]
                                 .rearrange("p (c u) -> p c u", u=16),
                                 drain=(hp == 0),
                                 inc="dv" if hp == 1 else None)
            self._p1_done[g] = last
        self._p1_feed = _p1_feed
        asmA = SB("p1_asmA", (128, 32 * 128), BF16)
        asmB = SB("p1_asmB", (128, 32 * 128), BF16)
        gatet = SB("p1_gatet", (128, 4 * PT), BF16)
        lrsp = SB("p1_lrsp", (HD, 4 * PT), F32)

        em, wt = self.emit, self.wait
        em("gpsimd", "memset", tails[:], 0.0, inc="gp")
        v_tails0 = self.cnt["gp"]
        wt("vector", "gp", v_tails0)

        prev_dma = None
        for ti in range(NPT):
            t0 = ti * PT
            v_hid = em("sync", "dma_start", inc="syB", incv=16,
                       out=hidt[:].rearrange("p (a t) -> p a t", a=8),
                       in_=d["hid_T"][:, t0:t0 + PT].rearrange("(a p) t -> p a t", p=128),
                       waits=([("gd", prev_dma)] if prev_dma else []))
            if prev_dma:
                wt("vector", "gd", prev_dma)

            for a in range(8):
                v_sq = em("scalar", "activation",
                          sqt[:, a * PT:(a + 1) * PT], hidt[:, a * PT:(a + 1) * PT],
                          AF.Square,
                          waits=([("syB", v_hid)] if a == 0 else []),
                          inc="ac" if a == 7 else None)
            wt("tensor", "syB", v_hid)
            wt("tensor", "ac", v_sq)
            for a in range(8):
                em("tensor", "matmul", pb[0][0:1, :], t["onesc_sb"][:],
                   hidt[:, a * PT:(a + 1) * PT], start=(a == 0), stop=(a == 7))
            for a in range(8):
                v_sum = em("tensor", "matmul", pb[1][0:1, :], t["onesc_sb"][:],
                           sqt[:, a * PT:(a + 1) * PT], start=(a == 0),
                           stop=(a == 7), inc="pe" if a == 7 else None)
            wt("vector", "pe", v_sum)
            em("vector", "tensor_scalar_mul", stat[0:1, 0:PT], pb[0][0:1, :], 1.0 / DM)
            em("vector", "tensor_mul", stat[0:1, 3 * PT:4 * PT],
               stat[0:1, 0:PT], stat[0:1, 0:PT], drain=True)
            v_var = em("vector", "scalar_tensor_tensor", stat[0:1, PT:2 * PT],
                       pb[1][0:1, :], 1.0 / DM, stat[0:1, 3 * PT:4 * PT],
                       ALU.mult, ALU.subtract, drain=True, inc="dv")
            wt("scalar", "dv", v_var)
            em("scalar", "activation", stat[0:1, 3 * PT:4 * PT],
               stat[0:1, PT:2 * PT], AF.Ln, bias=t["cb_sb"][0:1, 0:1])
            v_A = em("scalar", "activation", stat[0:1, PT:2 * PT],
                     stat[0:1, 3 * PT:4 * PT], AF.Exp, scale=-0.5,
                     drain=True, inc="ac")
            wt("vector", "ac", v_A)
            v_B = em("vector", "scalar_tensor_tensor", stat[0:1, 2 * PT:3 * PT],
                     stat[0:1, 0:PT], -1.0, stat[0:1, PT:2 * PT],
                     ALU.mult, ALU.mult, inc="dv")
            wt("tensor", "dv", v_B)
            em("tensor", "matmul", pb[4][:, :], t["onesr_sb"][:],
               stat[0:1, PT:2 * PT], start=True, stop=True)
            v_bc = em("tensor", "matmul", pb[5][:, :], t["onesr_sb"][:],
                      stat[0:1, 2 * PT:3 * PT], start=True, stop=True, inc="pe")
            wt("vector", "pe", v_bc)
            for a in range(8):
                em("vector", "tensor_mul", sqt[:, a * PT:(a + 1) * PT],
                   hidt[:, a * PT:(a + 1) * PT], pb[4][:, :],
                   drain=(a == 0))
                em("vector", "tensor_add", ln1t[:, a * PT:(a + 1) * PT],
                   sqt[:, a * PT:(a + 1) * PT], pb[5][:, :], drain=True)
            v_ln1 = self.emit("vector", "tensor_copy", ln1t[:, 0:1], ln1t[:, 0:1],
                              drain=True, inc="dv")

            wt("tensor", "dv", v_ln1)
            for a in range(8):
                v_lrmm = em("tensor", "matmul", pb[2][0:64, :],
                            t["wls_sb"][:, a * 64:(a + 1) * 64],
                            ln1t[:, a * PT:(a + 1) * PT],
                            start=(a == 0), stop=(a == 7),
                            inc="pe" if a == 7 else None)
            wt("scalar", "pe", v_lrmm)
            em("scalar", "activation", lrsp[:, 0:PT], pb[2][0:64, :], AF.Abs,
               bias=t["cb_sb"][0:64, 1:2])
            em("scalar", "activation", lrsp[:, PT:2 * PT], lrsp[:, 0:PT],
               AF.Exp, scale=-1.0, drain=True)
            v_sp = em("scalar", "activation", lrsp[:, 2 * PT:3 * PT],
                      lrsp[:, PT:2 * PT], AF.Ln, bias=t["cb_sb"][0:64, 2:3],
                      drain=True, inc="ac")
            wt("vector", "ac", v_sp)
            em("vector", "tensor_scalar", lrsp[:, 3 * PT:4 * PT], pb[2][0:64, :],
               0.001, 0.0, ALU.add, ALU.max)
            v_lrC = em("vector", "tensor_add", t["lrC_sb"][:, t0:t0 + PT],
                       lrsp[:, 3 * PT:4 * PT], lrsp[:, 2 * PT:3 * PT],
                       drain=True, inc="dv")

            projs = (("q", t["wq_sb"], t["convq_sb"], True),
                     ("k", t["wk_sb"], t["convk_sb"], True),
                     ("v", t["wv_sb"], t["convv_sb"], False),
                     ("g", t["wg_sb"], None, False))
            wt("tensor", "dv", v_lrC)
            if ti == 0:
                self._p1_evac = {}
                self._p1_done = {}
                self._p1_ns = {}
            evac = self._p1_evac
            gdone = self._p1_done
            pend = [None]

            def emit_aux(fn):
                if pend[0] is not None:
                    pend[0]()
                pend[0] = fn

            base_g = ti * 16
            for pi, (pname, wsb, csb, do_norm) in enumerate(projs):
                for f in range(4):
                    g = base_g + pi * 4 + f
                    sl = g % 2
                    pp = pb[2 + sl]
                    if g - 2 in evac:
                        wt("tensor", "dv", evac[g - 2])
                    for a in range(8):
                        v_pj = em("tensor", "matmul", pp[:, :],
                                  wsb[:, a * 512 + f * 128: a * 512 + (f + 1) * 128],
                                  ln1t[:, a * PT:(a + 1) * PT],
                                  start=(a == 0), stop=(a == 7),
                                  inc="pe" if a == 7 else None)
                    emit_aux(None) if False else None
                    if pend[0] is not None:
                        pend[0]()
                        pend[0] = None
                    if pname == "g":
                        wt("vector", "pe", v_pj)
                        evac[g] = em("vector", "tensor_copy",
                                     gatet[:, f * PT:(f + 1) * PT], pp[:, :],
                                     inc="dv")
                        continue
                    idx = pi * 4 + f
                    cvh = convh[:, sl * (PT + 3):(sl + 1) * (PT + 3)]
                    cv2 = cvt[:, sl * PT:(sl + 1) * PT]
                    sil = silt[:, sl * PT:(sl + 1) * PT]
                    sq2 = sq2t[:, sl * PT:(sl + 1) * PT]
                    fin = fint[:, sl * PT:(sl + 1) * PT]
                    rn2 = rnorm[:, sl * 2 * PT:(sl + 1) * 2 * PT]
                    wt("vector", "pe", v_pj)
                    if g - 2 in gdone:
                        wt("vector", "dv", gdone[g - 2])
                    em("vector", "tensor_copy", cvh[:, 0:3],
                       tails[:, idx * 3:idx * 3 + 3])
                    evac[g] = em("vector", "tensor_copy", cvh[:, 3:PT + 3],
                                 pp[:, :], inc="dv")
                    em("vector", "tensor_copy", tails[:, idx * 3:idx * 3 + 3],
                       cvh[:, PT:PT + 3], drain=True)
                    em("vector", "tensor_scalar_mul", cv2,
                       cvh[:, 3:PT + 3], csb[:, f * 5 + 4:f * 5 + 5], drain=True)
                    em("vector", "scalar_tensor_tensor", cv2,
                       cvh[:, 2:PT + 2], csb[:, f * 5 + 2:f * 5 + 3], cv2,
                       ALU.mult, ALU.add, drain=True)
                    em("vector", "scalar_tensor_tensor", cv2,
                       cvh[:, 1:PT + 1], csb[:, f * 5 + 1:f * 5 + 2], cv2,
                       ALU.mult, ALU.add, drain=True)
                    v_cv = em("vector", "scalar_tensor_tensor", cv2,
                              cvh[:, 0:PT], csb[:, f * 5 + 0:f * 5 + 1], cv2,
                              ALU.mult, ALU.add, drain=True, inc="dv")
                    wt("scalar", "dv", v_cv)
                    v_sil = em("scalar", "activation", sil, cv2, AF.Silu,
                               inc="ac")
                    if do_norm:
                        if g - 2 in self._p1_ns:
                            wt("scalar", "pe", self._p1_ns[g - 2])
                        v_sq2 = em("scalar", "activation", sq2, sil,
                                   AF.Square, drain=True, inc="ac")

                        def aux_norm(g=g, sl=sl, v_sq2=v_sq2, v_sil=v_sil,
                                     sq2=sq2, sil=sil, fin=fin, rn2=rn2,
                                     pname=pname, f=f):
                            pbank = pb[4] if sl == 0 else pb[5]
                            wt("tensor", "ac", v_sq2)
                            if g - 2 in self._p1_done:
                                wt("tensor", "dv", self._p1_done[g - 2])
                            v_ns = em("tensor", "matmul", pbank[0:2, :],
                                      t["halfsel_sb"][:], sq2,
                                      start=True, stop=True, inc="pe")
                            self._p1_ns[g] = v_ns
                            wt("scalar", "pe", v_ns)
                            em("scalar", "activation", rn2[0:2, 0:PT],
                               pbank[0:2, :], AF.Ln)
                            v_rn = em("scalar", "activation", rn2[0:2, PT:2 * PT],
                                      rn2[0:2, 0:PT], AF.Exp, scale=-0.5,
                                      drain=True, inc="ac")
                            wt("tensor", "ac", v_rn)
                            v_nb = em("tensor", "matmul", pbank[0:128, :],
                                      t["halfbc_sb"][:], rn2[0:2, PT:2 * PT],
                                      start=True, stop=True, inc="pe")
                            wt("vector", "ac", v_sil)
                            wt("vector", "pe", v_nb)
                            em("vector", "tensor_mul", fin, sil, pbank[0:128, :],
                               drain=True)
                            self._p1_feed(g, pname, f, fin)
                        pend[0] = aux_norm
                    else:
                        def aux_v(g=g, sl=sl, v_sil=v_sil, sil=sil, fin=fin,
                                  f=f, t0=t0):
                            pbank = pb[4] if sl == 0 else pb[5]
                            wt("tensor", "ac", v_sil)
                            if g - 2 in self._p1_done:
                                wt("tensor", "dv", self._p1_done[g - 2])
                            v_l2 = em("tensor", "matmul", pbank[0:128, :],
                                      t["lrsel_sb"][:, f * 128:(f + 1) * 128],
                                      t["lrC_sb"][0:32, t0:t0 + PT],
                                      start=True, stop=True, inc="pe")
                            wt("vector", "ac", v_sil)
                            self._p1_feed(g, "v", f, sil)
                            wt("vector", "pe", v_l2)
                            em("vector", "tensor_mul", fin, sil, pbank[0:128, :],
                               drain=True)
                            self._p1_feed(g, "vl", f, fin)
                        pend[0] = aux_v
            if pend[0] is not None:
                pend[0]()
                pend[0] = None

            v_all = self.cnt["dv"]
            wt("gpsimd", "dv", v_all)
            em("gpsimd", "dma_start", out=self.sc_feedA.ap()[:, ti * 4096:(ti + 1) * 4096],
               in_=asmA[:, :], inc="gd", incv=16)
            em("gpsimd", "dma_start", out=self.sc_feedB.ap()[:, ti * 4096:(ti + 1) * 4096],
               in_=asmB[:, :], inc="gd", incv=16)
            for f in range(4):
                prev_dma = em("gpsimd", "dma_start",
                              out=self.sc_gate.ap()[f * 128:(f + 1) * 128, t0:t0 + PT],
                              in_=gatet[:, f * PT:(f + 1) * PT],
                              inc="gd", incv=16)
        if "feedA" in self.taps or "feedB" in self.taps or "gate" in self.taps \
           or "lrC" in self.taps:
            wt("gpsimd", "gd", self.cnt["gd"])
            if "feedA" in self.taps:
                em("gpsimd", "dma_start", out=self.taps["feedA"],
                   in_=self.sc_feedA.ap()[:, :], inc="gd", incv=16)
            if "feedB" in self.taps:
                em("gpsimd", "dma_start", out=self.taps["feedB"],
                   in_=self.sc_feedB.ap()[:, :], inc="gd", incv=16)
            if "gate" in self.taps:
                em("gpsimd", "dma_start", out=self.taps["gate"],
                   in_=self.sc_gate.ap()[:, :], inc="gd", incv=16)
            if "lrC" in self.taps:
                em("gpsimd", "dma_start", out=self.taps["lrC"],
                   in_=t["lrC_sb"][:, :], inc="gd", incv=16)
            wt("gpsimd", "gd", self.cnt["gd"])
        self.barrier()
        self.flush('p1')
        self.end_scope()

    def _p2(self, es):
        t, d, pb, pbf = self.t, self.din, self.pb, self.pbf
        em, wt, SB = self.emit, self.wait, self.SB
        self.phase_scope()

        feedK = SB("p2_feedK", (64, 2 * SW * 128), BF16)
        feedV = SB("p2_feedV", (64, 2 * SW * 128), BF16)
        feedVL = SB("p2_feedVL", (64, 2 * SW * 128), BF16)
        feedQ = SB("p2_feedQ", (64, 2 * SW * 128), BF16)
        o_w = SB("p2_ow", (HD, 2 * OW * 128), F32)
        kvt = SB("p2_kvt", (128, 2 * 128), BF16)
        vlt = SB("p2_vlt", (128, 2 * 64), BF16)
        khqh = SB("p2_khqh", (128, 2 * 64), BF16)
        khqhT = SB("p2_khqhT", (32, 2 * 256), BF16)
        wpD = SB("p2_wpD", (32, 2 * 64), BF16)
        qkmd = SB("p2_qkmd", (128, 2 * 128), BF16)
        ppair = SB("p2_ppair", (HD, 128), BF16)
        pT = SB("p2_pT", (128, 64), BF16)
        qkm = SB("p2_qkm", (128, 128), BF16)
        sxm = SB("p2_sxm", (128, 64), F32)
        sxe = SB("p2_sxe", (128, 64), F32)
        sgm = SB("p2_sgm", (128, 64), F32)
        ssx = SB("p2_ssx", (HD, 128), F32)
        ssx2 = SB("p2_ssx2", (HD, 128), F32)
        sse = SB("p2_sse", (HD, 128), F32)
        ssm = SB("p2_ssm", (HD, 4), F32)

        wp, wpbf = t["wp_sb"], t["wpbf_sb"]
        idb = t["identb_sb"]
        lrC = t["lrC_sb"]
        wcst = SB("p2_wcst", (128, 2 * 8192), BF16)
        wc_jobs = []
        for (src, dst, rows, cols) in (("w1_T", self.sc_w1, DM, 4096),
                                       ("w2_T", self.sc_w2, DM, 4096),
                                       ("h1_T", self.sc_h1, 4096, 4096),
                                       ("h2_T", self.sc_h2, 4096, 4096),
                                       ("w3_T", self.sc_w3, 4096, DM)):
            for ch in range(rows // 256):
                wc_jobs.append((src, dst, ch, cols))
        self._wc_i = 0

        def emit_wcast():
            i = self._wc_i
            if i >= len(wc_jobs):
                return
            self._wc_i += 1
            srcn, dstt, ch, cols = wc_jobs[i]
            sl = i % 2
            stg = wcst[:, sl * 8192:sl * 8192 + 2 * cols]
            em("gpsimd", "dma_start", inc="wcL", incv=16,
               out=stg.rearrange("p (a u) -> p a u", a=2),
               in_=d[srcn][ch * 256:(ch + 1) * 256, :]
               .rearrange("(a p) u -> p a u", p=128),
               waits=([("wcS", (i - 1) * 16)] if i >= 2 else []))
            em("gpsimd", "dma_start", inc="wcS", incv=16,
               out=dstt.ap()[ch * 256:(ch + 1) * 256, :]
               .rearrange("(a p) u -> p a u", p=128),
               in_=stg.rearrange("p (a u) -> p a u", a=2),
               waits=[("wcL", (i + 1) * 16)])

        NW = NCH // SW
        win_sy = [0] * NW
        win_rel = [0, 0]
        o_rel = [0, 0]

        def load_window(w):
            wi = w % 2
            sem = "sy" if wi == 0 else "syB"
            cs = slice(wi * SW * 128, (wi + 1) * SW * 128)
            ds = slice(w * SW * 128, (w + 1) * SW * 128)
            em("sync", "dma_start", inc=sem, incv=16,
               out=feedK[:, cs], in_=self.sc_feedA.ap()[0:64, ds],
               waits=([("pe", win_rel[wi])] if win_rel[wi] else []))
            em("sync", "dma_start", inc=sem, incv=16,
               out=feedV[:, cs], in_=self.sc_feedA.ap()[64:128, ds])
            em("sync", "dma_start", inc=sem, incv=16,
               out=feedVL[:, cs], in_=self.sc_feedB.ap()[0:64, ds])
            win_sy[w] = (sem, em("sync", "dma_start", inc=sem, incv=16,
                                 out=feedQ[:, cs],
                                 in_=self.sc_feedB.ap()[64:128, ds]))

        wt("sync", "gd", self.cnt["gd"])
        load_window(0)
        if NW > 1:
            load_window(1)

        v_state = self.cnt["dv"]
        v_wpd_prev = 0
        prev = None
        hist = {}

        def emit_out_mm3(pv):
            wt("tensor", "ac", pv["v_kqt"])
            if pv["v_qkm_prev"]:
                wt("tensor", "dv", pv["v_qkm_prev"])
            pv["v_mm3"] = em("tensor", "matmul", pb[4][0:128, 0:128],
                             khqhT[0:32, pv["sl"] * 256:pv["sl"] * 256 + 128],
                             khqhT[0:32, pv["sl"] * 256 + 128:pv["sl"] * 256 + 256],
                             start=True, stop=True, inc="pe")

        def emit_out_qkm(pv):
            wt("vector", "pe", pv["v_mm3"])
            pv["v_qkm"] = em("vector", "tensor_mul",
                             qkmd[:, pv["sl"] * 128:(pv["sl"] + 1) * 128],
                             pb[4][0:128, 0:128], t["mtril_sb"][:], inc="dv")

        def emit_out_o(pv):
            wt("tensor", "dv", pv["v_qkm"])
            wt("tensor", "ac", max(pv["v_kv"], pv["v_wpd"]))
            if pv["v_oev_prev"]:
                wt("tensor", "ac", pv["v_oev_prev"])
            if pv["o_rel_wait"]:
                wt("tensor", pv["o_rel_wait"][0], pv["o_rel_wait"][1])
            em("tensor", "matmul", pb[3][0:64, 0:128],
               vlt[:, pv["sl"] * 64:(pv["sl"] + 1) * 64],
               qkmd[:, pv["sl"] * 128:(pv["sl"] + 1) * 128],
               start=True, stop=False)
            pv["v_o"] = em("tensor", "matmul", pb[3][0:64, 0:128],
                           wpD[:, pv["sl"] * 64:(pv["sl"] + 1) * 64],
                           khqhT[0:32, pv["sl"] * 256 + 128:pv["sl"] * 256 + 256],
                           start=False, stop=True, inc="pe")
            hist[pv["c"]]["v_o"] = pv["v_o"]

        def emit_out_evac(pv):
            wt("scalar", "pe", pv["v_o"])
            pv["v_oev"] = em("scalar", "copy",
                             o_w[:, pv["ocol"]:pv["ocol"] + 128],
                             pb[3][0:64, 0:128], inc="ac")

        def emit_oflush(pv):
            c1 = pv["c"]
            if (c1 + 1) % OW == 0:
                wo_i1 = (c1 // OW) % 2
                wt("gpsimd", "ac", pv["v_oev"])
                o_rel[wo_i1] = em("gpsimd", "dma_start",
                                  out=self.sc_o.ap()[:, (c1 + 1 - OW) * 128:(c1 + 1) * 128],
                                  in_=o_w[:, wo_i1 * OW * 128:(wo_i1 + 1) * OW * 128],
                                  inc="of" if wo_i1 == 0 else "ofB", incv=16)

        for c in range(NCH):
            w = c // SW
            wi = w % 2
            sl = c % 2
            osl = 1 - sl
            cb = wi * SW * 128 + (c % SW) * 128
            kT = feedK[:, cb:cb + 128]
            vT = feedV[:, cb:cb + 128]
            vlT = feedVL[:, cb:cb + 128]
            qT = feedQ[:, cb:cb + 128]
            wo_i = (c // OW) % 2
            ocol = wo_i * OW * 128 + (c % OW) * 128

            wt("tensor", win_sy[w][0], win_sy[w][1])
            if c >= 2:
                wt("tensor", "dv", hist[c - 2]["rel_kv"])
                wt("tensor", "ac", hist[c - 2]["v_kv"])
            em("tensor", "transpose", pbf[0][:, 0:64], kT, idb[0:64, 0:64])
            em("tensor", "transpose", pbf[0][:, 64:128], vT, idb[0:64, 0:64])
            v_t1 = em("tensor", "transpose", pbf[0][:, 128:192],
                      vlT, idb[0:64, 0:64], inc="pe")
            wt("scalar", "pe", v_t1)
            if c >= 2 and "v_o" in hist[c - 2]:
                wt("scalar", "pe", hist[c - 2]["v_o"])
            em("scalar", "copy", kvt[:, sl * 128:(sl + 1) * 128],
               pbf[0][:, 0:128])
            v_kv = em("scalar", "copy", vlt[:, sl * 64:(sl + 1) * 64],
                      pbf[0][:, 128:192], inc="ac")

            wt("tensor", "dv", v_state)
            if v_wpd_prev:
                wt("tensor", "ac", v_wpd_prev)
            em("tensor", "matmul", pb[0][0:128, 0:32], kT, wpbf[:, 0:32],
               start=True, stop=True)
            em("tensor", "matmul", pb[0][0:128, 32:64], qT, wpbf[:, 0:32],
               start=True, stop=True)
            em("tensor", "matmul", pb[1][0:32, 0:128], wpbf[:, 0:32], kT,
               start=True, stop=True)
            v_t5 = em("tensor", "transpose", pbf[1][0:32, 256:320],
                      wpbf[:, 32:64], idb[0:64, 0:64], inc="pe")
            wt("scalar", "pe", v_t5)
            v_wpd = em("scalar", "copy", wpD[:, sl * 64:(sl + 1) * 64],
                       pbf[1][0:32, 256:320], inc="ac")
            v_wpd_prev = v_wpd

            if prev is not None:
                emit_out_mm3(prev)

            wt("vector", "pe", v_t5)
            em("vector", "reduce_max",
               sgm[:, 0:16].rearrange("p (g o) -> p g o", o=1),
               pb[0][0:128, 0:64].rearrange("p (g u) -> p g u", u=4),
               axis=AX.X, negate=True, drain=True)
            v_sf1 = em("vector", "tensor_add",
                       sxm[:].rearrange("p (g u) -> p g u", u=4),
                       pb[0][0:128, 0:64].rearrange("p (g u) -> p g u", u=4),
                       sgm[:, 0:16].rearrange("p (g o) -> p g o", o=1)
                       .broadcast_to((128, 16, 4)),
                       drain=True, inc="dv")
            wt("scalar", "dv", v_sf1)
            v_exp1 = em("scalar", "activation", sxe[:], sxm[:], AF.Exp, inc="ac")
            if prev is not None:
                emit_out_qkm(prev)
            wt("vector", "ac", v_exp1)
            em("vector", "reduce_sum",
               sgm[:, 16:32].rearrange("p (g o) -> p g o", o=1),
               sxe[:].rearrange("p (g u) -> p g u", u=4), axis=AX.X, drain=True)
            em("vector", "reciprocal", sgm[:, 32:48], sgm[:, 16:32], drain=True)
            em("vector", "tensor_mul", sgm[:, 48:64], sgm[:, 32:48],
               t["bd16_sb"][:], drain=True)
            v_kh = em("vector", "tensor_mul",
                      khqh[:, sl * 64:(sl + 1) * 64]
                      .rearrange("p (g u) -> p g u", u=4),
                      sxe[:].rearrange("p (g u) -> p g u", u=4),
                      sgm[:, 48:64].rearrange("p (g o) -> p g o", o=1)
                      .broadcast_to((128, 16, 4)), drain=True, inc="dv")

            wt("tensor", "dv", v_kh)
            wt("tensor", "ac", v_kv)
            v_mm6 = em("tensor", "matmul", pb[2][0:64, 0:32],
                       vlt[:, sl * 64:(sl + 1) * 64],
                       khqh[:, sl * 64:sl * 64 + 32], start=True, stop=True,
                       inc="pe")
            if prev is not None:
                emit_out_o(prev)
            wt("vector", "pe", v_mm6)
            v_wout1 = em("vector", "tensor_add", wpbf[:, 32:64], wp[:, 32:64],
                         pb[2][0:64, 0:32], drain=True, inc="dv")
            em("vector", "tensor_add", wp[:, 32:64], wp[:, 32:64],
               pb[2][0:64, 0:32])

            wt("tensor", "dv", v_kh)
            em("tensor", "transpose", pbf[1][0:32, 0:128],
               khqh[:, sl * 64:sl * 64 + 32], idb[:])
            v_t2 = em("tensor", "transpose", pbf[1][0:32, 128:256],
                      khqh[:, sl * 64 + 32:sl * 64 + 64], idb[:], inc="pe")
            wt("scalar", "pe", v_t2)
            v_kqt = em("scalar", "copy",
                       khqhT[:, sl * 256:(sl + 1) * 256],
                       pbf[1][0:32, 0:256], inc="ac")

            for it in range(2):
                if it == 0:
                    wt("tensor", "dv", v_wout1)
                    v_mm9 = em("tensor", "matmul", pb[1][32:64, 0:128],
                               wpbf[:, 32:64], vT, start=True, stop=True,
                               inc="pe")
                else:
                    wt("tensor", "dv", v_st1)
                    em("tensor", "matmul", pb[1][0:32, 0:128], wpbf[:, 0:32],
                       kT, start=True, stop=True)
                    v_mm9 = em("tensor", "matmul", pb[1][32:64, 0:128],
                               wpbf[:, 32:64], vT, start=True, stop=True,
                               inc="pe")
                wt("vector", "pe", v_mm9)
                em("vector", "tensor_add", ssx[:], pb[1][0:64, 0:128],
                   t["smask_sb"][:])
                em("vector", "reduce_max", ssm[:, 0:1], ssx[:], axis=AX.X,
                   negate=True, drain=True)
                v_sp = em("vector", "tensor_scalar_mul", ssm[:, 3:4],
                          ssm[:, 0:1], 0.125, drain=True, inc="dv")
                wt("scalar", "dv", v_sp)
                v_se = em("scalar", "activation", sse[:], ssx[:], AF.Exp,
                          scale=0.125, bias=ssm[:, 3:4],
                          accum_out=ssm[:, 1:2], inc="ac")
                if it == 0 and prev is not None:
                    emit_out_evac(prev)
                    emit_oflush(prev)
                wt("vector", "ac", v_se)
                em("vector", "reciprocal", ssm[:, 2:3], ssm[:, 1:2])
                v_pp = em("vector", "tensor_scalar", ppair[:], sse[:],
                          ssm[:, 2:3], lrC[:, CH * c:CH * c + 1],
                          ALU.mult, ALU.mult, drain=True, inc="dv")
                wt("tensor", "dv", v_pp)
                v_t4 = em("tensor", "transpose", pbf[1][0:128, 320:384],
                          ppair[:], idb[0:64, 0:64], inc="pe")
                wt("scalar", "pe", v_t4)
                v_pt = em("scalar", "copy", pT[:], pbf[1][0:128, 320:384],
                          inc="ac")
                wt("tensor", "ac", v_pt)
                em("tensor", "matmul", pb[2][0:64, 64:96],
                   kvt[:, sl * 128 + 64:sl * 128 + 128],
                   pT[:, 0:32], start=True, stop=True)
                v_a = em("tensor", "matmul", pb[2][0:64, 32:64],
                         kvt[:, sl * 128:sl * 128 + 64],
                         pT[:, 32:64], start=True, stop=True, inc="pe")
                wt("vector", "pe", v_a)
                v_st1 = em("vector", "tensor_add", wpbf[:], wp[:],
                           pb[2][0:64, 32:96], drain=True, inc="dv")
                em("vector", "tensor_add", wp[:, 0:64], wp[:, 0:64],
                   pb[2][0:64, 32:96])
            v_state = v_st1

            pv = {"c": c, "sl": sl, "v_kqt": v_kqt, "ocol": ocol,
                  "v_kv": v_kv, "v_wpd": v_wpd,
                  "v_qkm_prev": prev["v_qkm"] if prev else 0,
                  "v_oev_prev": prev["v_oev"] if prev else 0,
                  "o_rel_wait": (("of" if wo_i == 0 else "ofB", o_rel[wo_i])
                                 if o_rel[wo_i] else None)}
            hist[c] = {"rel_kv": v_st1, "v_kv": v_kv}
            prev = pv

            if c % 3 == 0:
                emit_wcast()
            if (c + 1) % SW == 0:
                win_rel[wi] = self.cnt["pe"]
                if w + 2 < NW:
                    load_window(w + 2)

        emit_out_mm3(prev)
        emit_out_qkm(prev)
        emit_out_o(prev)
        emit_out_evac(prev)
        emit_oflush(prev)
        while self._wc_i < len(wc_jobs):
            emit_wcast()

        if "o" in self.taps:
            wt("gpsimd", "gd", self.cnt["gd"])
            em("gpsimd", "dma_start", out=self.taps["o"],
               in_=self.sc_o.ap()[:, :], inc="gd", incv=16)
        self.barrier()
        self.flush('p2')
        self.end_scope()
        self._es_mixer.close()

    def _p3(self, es):
        import dataclasses
        t, d, pb, pbf = self.t, self.din, self.pb, self.pbf
        em, wt, SB = self.emit, self.wait, self.SB
        self.phase_scope()

        from contextlib import ExitStack
        es_a = ExitStack()
        outer_es = self._cur_es
        self._cur_es = es_a
        o_seg = SB("p3_oseg", (HD, 4096), BF16)
        sqo = SB("p3_sqo", (HD, 512), F32)
        o2t = SB("p3_o2t", (HD, 512), F32)
        ostat = SB("p3_ostat", (1, 4 * 512), F32)
        o_n = SB("p3_on", (HD, 512), BF16)
        gate_w = SB("p3_gatew", (128, 4 * 512), BF16)
        ogbuf = SB("p3_ogbuf", (128, 4 * 512), BF16)
        ogbuf2 = SB("p3_ogbuf2", (128, 4 * 512), BF16)
        onesb64 = SB("p3_ones64b", (HD, 1), BF16)
        em("gpsimd", "memset", onesb64[:], 1.0, inc="gp")
        wt("tensor", "gp", self.cnt["gp"])

        wt("sync", "of", self.cnt["of"])
        wt("sync", "ofB", self.cnt["ofB"])
        prev_seg = 0
        v_gmark = 0
        for seg in range(NPT):
            t0 = seg * PT
            v_os = em("sync", "dma_start", inc="sy", incv=16,
                      out=o_seg[:], in_=self.sc_o.ap()[:, seg * 4096:(seg + 1) * 4096],
                      waits=([("gd", prev_seg)] if prev_seg else []))
            for f in range(4):
                v_gw = em("sync", "dma_start", inc="sy", incv=16,
                          out=gate_w[:, f * PT:(f + 1) * PT],
                          in_=self.sc_gate.ap()[f * 128:(f + 1) * 128, t0:t0 + PT])
            wt("scalar", "sy", v_gw)
            wt("tensor", "sy", v_gw)
            for g in range(8):
                osl = o_seg[:, g * 512:(g + 1) * 512]
                v_sq = em("scalar", "activation", sqo[:], osl, AF.Square,
                          drain=(g > 0), inc="ac",
                          waits=([("dv", v_gmark)] if (g > 0 or seg > 0) else []))
                wt("tensor", "ac", v_sq)
                em("tensor", "matmul", pb[0][0:1, :], onesb64[:], osl,
                   start=True, stop=True)
                v_sums = em("tensor", "matmul", pb[1][0:1, :],
                            t["onesc_sb"][0:64, :], sqo[:],
                            start=True, stop=True, inc="pe")
                wt("vector", "pe", v_sums)
                em("vector", "tensor_scalar_mul", ostat[0:1, 0:512],
                   pb[0][0:1, :], 1.0 / HD, drain=True)
                em("vector", "tensor_mul", ostat[0:1, 1536:2048],
                   ostat[0:1, 0:512], ostat[0:1, 0:512], drain=True)
                v_var = em("vector", "scalar_tensor_tensor", ostat[0:1, 512:1024],
                           pb[1][0:1, :], 1.0 / HD, ostat[0:1, 1536:2048],
                           ALU.mult, ALU.subtract, drain=True, inc="dv")
                wt("scalar", "dv", v_var)
                em("scalar", "activation", ostat[0:1, 1536:2048],
                   ostat[0:1, 512:1024], AF.Ln, bias=t["cb_sb"][0:1, 0:1],
                   drain=True)
                v_A = em("scalar", "activation", ostat[0:1, 512:1024],
                         ostat[0:1, 1536:2048], AF.Exp, scale=-0.5,
                         drain=True, inc="ac")
                wt("vector", "ac", v_A)
                v_B = em("vector", "scalar_tensor_tensor", ostat[0:1, 1024:1536],
                         ostat[0:1, 0:512], -1.0, ostat[0:1, 512:1024],
                         ALU.mult, ALU.mult, inc="dv")
                wt("tensor", "dv", v_B)
                em("tensor", "matmul", pb[4][0:64, :], t["onesr_sb"][:, 0:64],
                   ostat[0:1, 512:1024], start=True, stop=True)
                v_bc = em("tensor", "matmul", pb[5][0:64, :],
                          t["onesr_sb"][:, 0:64], ostat[0:1, 1024:1536],
                          start=True, stop=True, inc="pe")
                wt("vector", "pe", v_bc)
                em("vector", "tensor_mul", sqo[:], osl, pb[4][0:64, :],
                   drain=True)
                em("vector", "tensor_add", o2t[:], sqo[:], pb[5][0:64, :],
                   drain=True)
                em("vector", "tensor_scalar", o_n[:], o2t[:], t["lng_sb"][:],
                   t["lnb_sb"][:], ALU.mult, ALU.add, drain=True)
                for f in range(4):
                    for hp in range(2):
                        lane = 2 * f + hp
                        v_gmark = em("vector", "tensor_copy",
                           ogbuf[64 * hp:64 * hp + 64, f * 512 + g * 64:
                                 f * 512 + (g + 1) * 64],
                           o_n[:].rearrange("p (cc l u) -> p cc l u", l=8, u=16)
                           [:, :, lane, :], drain=(f == 0 and hp == 0),
                           inc="dv" if (f, hp) == (3, 1) else None)
            if prev_seg:
                wt("vector", "gd", prev_seg)
            v_og = em("vector", "tensor_mul", ogbuf2[:], ogbuf[:], gate_w[:],
                      drain=True, inc="dv")
            wt("gpsimd", "dv", v_og)
            for f in range(4):
                prev_seg = em("gpsimd", "dma_start",
                              out=self.sc_og.ap()[f * 128:(f + 1) * 128, t0:t0 + PT],
                              in_=ogbuf2[:, f * PT:(f + 1) * PT],
                              inc="gd", incv=16)

        wt("gpsimd", "gd", self.cnt["gd"])
        v_cc = em("gpsimd", "collective_compute", "AllGather", ALU.bypass,
                  replica_groups=[[0, 1], [2, 3], [4, 5], [6, 7]],
                  ins=[self.sc_og.ap().opt()], outs=[self.og_full.ap().opt()],
                  inc="cc")
        if "og" in self.taps:
            wt("gpsimd", "cc", v_cc)
            em("gpsimd", "dma_start", out=self.taps["og"],
               in_=self.og_full.ap()[:, :], inc="gd", incv=16)

        es_a.close()
        self._cur_es = outer_es
        wo_sb = SB("p3_wo", (128, 8 * 1024), BF16)
        ogf = SB("p3_ogf", (128, 8 * 512), BF16)
        hidh = SB("p3_hidh", (128, 8 * 512), F32)
        xmid = SB("p3_xmid", (128, 8 * 512), F32)
        sq8 = hidh
        ln2t = SB("p3_ln2", (128, 8 * 512), BF16)
        lstat = SB("p3_lstat", (1, 4 * 512), F32)
        hbuf = SB("p3_h", (128, 32 * 512), BF16)
        hbuf2 = SB("p3_h2", (128, 32 * 512), BF16)
        w12s = SB("p3_w12s", (128, 4 * 2 * 8 * 128), BF16)
        whs = SB("p3_whs", (128, 4 * 32 * 128), BF16)
        ht1 = SB("p3_ht1", (128, 2 * 512), BF16)
        ht2 = SB("p3_ht2", (128, 2 * 512), F32)
        obuf = SB("p3_obuf", (128, 2 * 512), F32)

        v_woc = em("gpsimd", "dma_start", inc="gdO", incv=16,
                   out=wo_sb[:].rearrange("p (a r) -> p a r", a=8),
                   in_=d["wo_T"].rearrange("(a p) r -> p a r", p=128))

        def dyn_og_load(tl0, dst3, wv):
            def f(E, S, tl0=tl0, wv=wv):
                for sem, val in wv:
                    E.wait_ge(S[sem], val)
                if getattr(self, "_paroff", None) is None:
                    pid = E.partition_id()
                    self._paroff = E.compute_val((pid % 2) * TOKH)
                base = self.og_full.ap().rearrange("(a p) u -> p a u",
                                                   p=128)[:, :, tl0:tl0 + 512]
                dynap = dataclasses.replace(base,
                                            offset=base.offset + self._paroff)
                E.dma_start(out=dst3, in_=dynap).then_inc(S["gdO"], 16)
            self.st.ops["gpsimd"].append(f)
            self.cnt["gdO"] += 16
            return self.cnt["gdO"]

        v_ffn_rel = 0
        v_wrel = [0, 0]
        for tb in range(TOKH // FFB):
            tl0 = tb * FFB
            wvl = [("cc", v_cc)] + ([("dv", v_ffn_rel)] if v_ffn_rel else [])
            v_ogf = dyn_og_load(tl0, ogf[:].rearrange("p (a u) -> p a u", a=8),
                                wvl)
            v_hid = em("sync", "dma_start", inc="sy", incv=16,
                       out=hidh[:].rearrange("p (a u) -> p a u", a=8),
                       in_=d["hid_half"][:, tl0:tl0 + 512]
                       .rearrange("(a p) u -> p a u", p=128),
                       waits=([("dv", v_ffn_rel)] if v_ffn_rel
                              else [("cc", v_cc)]))
            wt("tensor", "gdO", max(v_ogf, v_woc))
            wt("tensor", "sy", v_hid)
            for m in range(8):
                pp = pb[2 + m % 2]
                if m >= 2:
                    wt("tensor", "dv", self.cnt["dv"])
                for a in range(8):
                    v_wo = em("tensor", "matmul", pp[:, :],
                              wo_sb[:, a * 1024 + m * 128:a * 1024 + (m + 1) * 128],
                              ogf[:, a * 512:(a + 1) * 512],
                              start=(a == 0), stop=(a == 7),
                              inc="pe" if a == 7 else None)
                wt("vector", "pe", v_wo)
                em("vector", "tensor_add", xmid[:, m * 512:(m + 1) * 512],
                   hidh[:, m * 512:(m + 1) * 512], pp[:, :], inc="dv")
            v_lnw = self.cnt["dv"]
            for a in range(8):
                v_sq = em("scalar", "activation", sq8[:, a * 512:(a + 1) * 512],
                          xmid[:, a * 512:(a + 1) * 512], AF.Square,
                          waits=([("dv", v_lnw)] if a == 0 else []),
                          inc="ac" if a == 7 else None)
            wt("tensor", "dv", v_lnw)
            wt("tensor", "ac", v_sq)
            for a in range(8):
                em("tensor", "matmul", pb[0][0:1, :], t["onesc_sb"][:],
                   xmid[:, a * 512:(a + 1) * 512], start=(a == 0), stop=(a == 7))
            for a in range(8):
                v_sums = em("tensor", "matmul", pb[1][0:1, :], t["onesc_sb"][:],
                            sq8[:, a * 512:(a + 1) * 512], start=(a == 0),
                            stop=(a == 7), inc="pe" if a == 7 else None)
            wt("vector", "pe", v_sums)
            em("vector", "tensor_scalar_mul", lstat[0:1, 0:512], pb[0][0:1, :],
               1.0 / DM, drain=True)
            em("vector", "tensor_mul", lstat[0:1, 1536:2048], lstat[0:1, 0:512],
               lstat[0:1, 0:512], drain=True)
            v_var = em("vector", "scalar_tensor_tensor", lstat[0:1, 512:1024],
                       pb[1][0:1, :], 1.0 / DM, lstat[0:1, 1536:2048],
                       ALU.mult, ALU.subtract, drain=True, inc="dv")
            wt("scalar", "dv", v_var)
            em("scalar", "activation", lstat[0:1, 1536:2048],
               lstat[0:1, 512:1024], AF.Ln, bias=t["cb_sb"][0:1, 0:1], drain=True)
            v_A = em("scalar", "activation", lstat[0:1, 512:1024],
                     lstat[0:1, 1536:2048], AF.Exp, scale=-0.5, drain=True,
                     inc="ac")
            wt("vector", "ac", v_A)
            v_B = em("vector", "scalar_tensor_tensor", lstat[0:1, 1024:1536],
                     lstat[0:1, 0:512], -1.0, lstat[0:1, 512:1024],
                     ALU.mult, ALU.mult, inc="dv")
            wt("tensor", "dv", v_B)
            em("tensor", "matmul", pb[4][:, :], t["onesr_sb"][:],
               lstat[0:1, 512:1024], start=True, stop=True)
            v_bc = em("tensor", "matmul", pb[5][:, :], t["onesr_sb"][:],
                      lstat[0:1, 1024:1536], start=True, stop=True, inc="pe")
            wt("vector", "pe", v_bc)
            for a in range(8):
                em("vector", "tensor_mul", sq8[:, a * 512:(a + 1) * 512],
                   xmid[:, a * 512:(a + 1) * 512], pb[4][:, :], drain=(a == 0))
                em("vector", "tensor_add", ln2t[:, a * 512:(a + 1) * 512],
                   sq8[:, a * 512:(a + 1) * 512], pb[5][:, :], drain=True)
            v_ln2 = self.emit("vector", "tensor_copy", ln2t[:, 0:1],
                              ln2t[:, 0:1], drain=True, inc="dv")

            WSEM = ("wg0", "wg1", "wg2", "wg3")
            WHSEM = ("wh0", "wh1", "wh2", "wh3")
            if tb == 0:
                wt("sync", "wcS", self.cnt["wcS"])
            if not hasattr(self, "_wrel"):
                self._wrel = {}
            wrel = self._wrel

            WS = {"w1_T": self.sc_w1, "w2_T": self.sc_w2, "h1_T": self.sc_h1,
                  "h2_T": self.sc_h2, "w3_T": self.sc_w3}

            def w12_dma(fo):
                sl = fo % 4
                woff = sl * 16 * 128
                wv = [("pe", wrel[sl])] if wrel.get(sl) else []
                em("sync", "dma_start", inc=WSEM[sl], incv=16,
                   out=w12s[:, woff:woff + 8 * 128]
                   .rearrange("p (a u) -> p a u", a=8),
                   in_=WS["w1_T"].ap()[:, fo * 128:(fo + 1) * 128]
                   .rearrange("(a p) u -> p a u", p=128), waits=wv)
                return em("sync", "dma_start", inc=WSEM[sl], incv=16,
                          out=w12s[:, woff + 8 * 128:woff + 16 * 128]
                          .rearrange("p (a u) -> p a u", a=8),
                          in_=WS["w2_T"].ap()[:, fo * 128:(fo + 1) * 128]
                          .rearrange("(a p) u -> p a u", p=128))

            def wh_dma(wname, fo):
                eng, sems = (("gpsimd", WHSEM) if wname in ("h1_T", "w3_T")
                             else ("sync", WSEM))
                sl = fo % 4
                woff = sl * 32 * 128
                wv = [("pe", wrel[sl])] if wrel.get(sl) else []
                return em(eng, "dma_start", inc=sems[sl], incv=16,
                          out=whs[:, woff:woff + 32 * 128]
                          .rearrange("p (a u) -> p a u", a=32),
                          in_=WS[wname].ap()[:, fo * 128:(fo + 1) * 128]
                          .rearrange("(a p) u -> p a u", p=128), waits=wv)

            vdma = {}
            for fo in range(4):
                vdma[fo] = w12_dma(fo)
            v_sil = {}
            v_h = {}
            for fo in range(32):
                sl = fo % 4
                woff = sl * 16 * 128
                pA = pb[2 + 2 * (fo % 2)]
                pB = pb[3 + 2 * (fo % 2)]
                wt("tensor", WSEM[sl], vdma[fo])
                if fo == 0:
                    wt("tensor", "dv", v_ln2)
                    wt("tensor", "ac", self.cnt["ac"])
                if fo >= 2:
                    wt("tensor", "ac", v_sil[fo - 2])
                    wt("tensor", "dv", v_h[fo - 2])
                for a in range(8):
                    em("tensor", "matmul", pA[:, :],
                       w12s[:, woff + a * 128:woff + (a + 1) * 128],
                       ln2t[:, a * 512:(a + 1) * 512],
                       start=(a == 0), stop=(a == 7))
                for a in range(8):
                    v_s1m = em("tensor", "matmul", pB[:, :],
                               w12s[:, woff + (8 + a) * 128:woff + (9 + a) * 128],
                               ln2t[:, a * 512:(a + 1) * 512],
                               start=(a == 0), stop=(a == 7),
                               inc="pe" if a == 7 else None)
                wrel[sl] = v_s1m
                if fo + 4 < 32:
                    vdma[fo + 4] = w12_dma(fo + 4)
                hsl = fo % 2
                wt("scalar", "pe", v_s1m)
                if fo >= 2:
                    wt("scalar", "dv", v_h[fo - 2])
                v_sil[fo] = em("scalar", "activation", ht1[:, hsl * 512:(hsl + 1) * 512],
                               pA[:, :], AF.Silu,
                               scale=t["s1_sb"][:, fo:fo + 1], inc="ac")
                wt("vector", "ac", v_sil[fo])
                em("vector", "tensor_scalar_mul", ht2[:, hsl * 512:(hsl + 1) * 512],
                   pB[:, :], t["s2_sb"][:, fo:fo + 1])
                v_h[fo] = em("vector", "tensor_mul",
                             hbuf[:, fo * 512:(fo + 1) * 512],
                             ht1[:, hsl * 512:(hsl + 1) * 512],
                             ht2[:, hsl * 512:(hsl + 1) * 512],
                             drain=True, inc="dv")

            for stage, (wsrc, ssrc, src, dst) in enumerate((
                    ("h1_T", "sh1_sb", hbuf, hbuf2),
                    ("h2_T", "sh2_sb", hbuf2, hbuf))):
                v_in = self.cnt["dv"] if stage == 0 else None
                v_in_ac = self.cnt["ac"]
                ws = WHSEM if wsrc in ("h1_T", "w3_T") else WSEM
                vdma = {}
                for fo in range(4):
                    vdma[fo] = wh_dma(wsrc, fo)
                v_g = {}
                for fo in range(32):
                    sl = fo % 4
                    woff = sl * 32 * 128
                    pp = pb[2 + sl]
                    wt("tensor", ws[sl], vdma[fo])
                    if fo == 0:
                        if v_in is not None:
                            wt("tensor", "dv", v_in)
                        else:
                            wt("tensor", "ac", v_in_ac)
                    if fo >= 4:
                        wt("tensor", "ac", v_g[fo - 4])
                    for a in range(32):
                        v_hm = em("tensor", "matmul", pp[:, :],
                                  whs[:, woff + a * 128:woff + (a + 1) * 128],
                                  src[:, a * 512:(a + 1) * 512],
                                  start=(a == 0), stop=(a == 31),
                                  inc="pe" if a == 31 else None)
                    wrel[sl] = v_hm
                    if fo + 4 < 32:
                        vdma[fo + 4] = wh_dma(wsrc, fo + 4)
                    wt("scalar", "pe", v_hm)
                    v_g[fo] = em("scalar", "activation",
                                 dst[:, fo * 512:(fo + 1) * 512], pp[:, :],
                                 AF.Gelu, scale=t[ssrc][:, fo:fo + 1],
                                 drain=True, inc="ac")

            v_h3 = self.cnt["ac"]
            wt("tensor", "ac", v_h3)
            v_ob = {}
            vdma = {}
            for m in range(4):
                vdma[m] = wh_dma("w3_T", m)
            for m in range(8):
                sl = m % 4
                woff = sl * 32 * 128
                pp = pb[2 + sl]
                wt("tensor", WHSEM[sl], vdma[m])
                if m >= 4:
                    wt("tensor", "dv", v_ob[m - 4])
                for a in range(32):
                    v_o4 = em("tensor", "matmul", pp[:, :],
                              whs[:, woff + a * 128:woff + (a + 1) * 128],
                              hbuf[:, a * 512:(a + 1) * 512],
                              start=(a == 0), stop=(a == 31),
                              inc="pe" if a == 31 else None)
                wrel[sl] = v_o4
                if m + 4 < 8:
                    vdma[m + 4] = wh_dma("w3_T", m + 4)
                wt("vector", "pe", v_o4)
                if m >= 2:
                    wt("vector", "so", v_so[m - 1])
                v_ob[m] = em("vector", "scalar_tensor_tensor",
                             obuf[:, (m % 2) * 512:(m % 2 + 1) * 512], pp[:, :],
                             t["s3_sb"][:, m:m + 1], xmid[:, m * 512:(m + 1) * 512],
                             ALU.mult, ALU.add, drain=True, inc="dv")
                wt("gpsimd", "dv", v_ob[m])
                if m == 0:
                    v_so = {}
                v_so[m] = em("gpsimd", "dma_start",
                             out=self.out[m * 128:(m + 1) * 128, tl0:tl0 + 512],
                             in_=obuf[:, (m % 2) * 512:(m % 2 + 1) * 512],
                             inc="so", incv=16)

            v_ffn_rel = self.cnt["dv"]
        if "xmid" in self.taps:
            wt("gpsimd", "dv", self.cnt["dv"])
            em("gpsimd", "dma_start", out=self.taps["xmid"],
               in_=xmid[:, :], inc="gd", incv=16)
        wt("gpsimd", "so", self.cnt["so"])
        self.barrier()
        self.flush('p3')
        self.end_scope()

    def _p4(self, es):
        pass


def build_kernel(debug_taps=()):
    kb = KB(debug_taps)
    nc = kb.build()
    return nc, kb


def kernel(**inputs):
    per_core = host_prep(inputs)
    nc, kb = build_kernel()
    res = run_bass_kernel_spmd(nc, per_core, core_ids=list(range(8)))
    full = np.zeros((4, L, DM), np.float32)
    for b in range(4):
        full[b, 0:TOKH, :] = np.asarray(res.results[2 * b]["out_T"]).T
        full[b, TOKH:L, :] = np.asarray(res.results[2 * b + 1]["out_T"]).T
    return full


# version 19
# speedup vs baseline: 1.0514x; 1.0514x over previous
import numpy as np
import concourse.bass as bass
import concourse.mybir as mybir
from concourse.bass_utils import run_bass_kernel_spmd

F32 = mybir.dt.float32
BF16 = mybir.dt.bfloat16
AF = mybir.ActivationFunctionType
ALU = mybir.AluOpType
AX = mybir.AxisListType

L = 4096
DM = 1024
NHC = 8
HD = 64
DI = 4
CH = 16
NCH = L // CH
PT = 512
NPT = L // PT
SW = 32
OW = 16
TOKH = L // 2
FFB = 512
NEG = -1.0e30
ENGINES = ("sync", "tensor", "vector", "scalar", "gpsimd")



def host_prep(inputs):
    hid = np.asarray(inputs["L0_attn_hidden_states"], np.float32)
    Wq = np.asarray(inputs["L0_attn_Wq"], np.float32)
    Wk = np.asarray(inputs["L0_attn_Wk"], np.float32)
    Wv = np.asarray(inputs["L0_attn_Wv"], np.float32)
    Wlr = np.asarray(inputs["L0_attn_Wlr"], np.float32)
    Wg = np.asarray(inputs["L0_attn_Wg"], np.float32)
    Wo = np.asarray(inputs["L0_attn_Wo"], np.float32)
    cq = np.asarray(inputs["L0_attn_cq"], np.float32)
    ck = np.asarray(inputs["L0_attn_ck"], np.float32)
    cv = np.asarray(inputs["L0_attn_cv"], np.float32)
    Wi0 = np.asarray(inputs["L0_attn_W_in_init"], np.float32)
    Wo0 = np.asarray(inputs["L0_attn_W_out_init"], np.float32)
    ln_g = np.asarray(inputs["L0_attn_ln_g"], np.float32)
    ln_b = np.asarray(inputs["L0_attn_ln_b"], np.float32)

    ident = np.eye(128, dtype=np.float32)
    halfsel = np.zeros((128, 2), np.float32)
    halfsel[:64, 0] = 1.0
    halfsel[64:, 1] = 1.0
    halfbc = np.zeros((2, 128), np.float32)
    halfbc[0, :64] = 1.0
    halfbc[1, 64:] = 1.0
    ones_col = np.ones((128, 1), np.float32)
    ones_row = np.ones((1, 128), np.float32)
    lrsel = np.zeros((32, 4 * 128), np.float32)
    for f in range(4):
        for pf in range(128):
            lrsel[4 * (2 * f + pf // 64), 128 * f + pf] = 1.0
    bd16 = np.zeros((128, 16), np.float32)
    for p in range(128):
        bd16[p, p // 16] = 1.0
        bd16[p, 8 + p // 16] = 1.0
    smask = np.full((64, 128), NEG, np.float32)
    for p in range(64):
        lane = (p % 32) // 4
        smask[p, lane * 16:(lane + 1) * 16] = 0.0
    smk_l = np.zeros((9, 64), np.float32)
    smk_r = np.zeros((9, 128), np.float32)
    smk_l[0, :] = -1.0e30
    smk_r[0, :] = 1.0
    for l in range(8):
        for p in range(64):
            if (p % 32) // 4 == l:
                smk_l[1 + l, p] = 1.0e30
        smk_r[1 + l, l * 16:(l + 1) * 16] = 1.0
    mtril = np.zeros((128, 128), np.float32)
    for tk in range(128):
        for tq in range(128):
            if tk // 16 == tq // 16 and (tq % 16) >= (tk % 16):
                mtril[tk, tq] = 1.0

    per_core = []
    for c in range(8):
        b, hh = c // 2, c % 2
        h0 = hh * NHC
        sl = slice(h0 * HD, h0 * HD + NHC * HD)
        m = {}
        m["hid_T"] = np.ascontiguousarray(hid[b].T)
        m["hid_half"] = np.ascontiguousarray(hid[b].T[:, hh * TOKH:(hh + 1) * TOKH])
        m["wq_T"] = np.ascontiguousarray(Wq[sl, :].T)
        m["wk_T"] = np.ascontiguousarray(Wk[sl, :].T)
        m["wv_T"] = np.ascontiguousarray(Wv[sl, :].T)
        m["wg_T"] = np.ascontiguousarray(Wg[sl, :].T)
        m["wo_T"] = np.ascontiguousarray(Wo.T)
        wls = np.zeros((DM, 64), np.float32)
        for p in range(32):
            wls[:, p] = Wlr[2 * (h0 + p // 4) + 1, :]
            wls[:, 32 + p] = Wlr[2 * (h0 + p // 4), :]
        m["wlr_scan"] = wls
        m["cq_s"] = np.ascontiguousarray(cq[sl, :])
        m["ck_s"] = np.ascontiguousarray(ck[sl, :])
        m["cv_s"] = np.ascontiguousarray(cv[sl, :])
        wp = np.zeros((HD, 64), np.float32)
        for lane in range(NHC):
            for D in range(DI):
                wp[:, lane * 4 + D] = Wi0[0, D, h0 + lane, :]
                wp[:, 32 + lane * 4 + D] = Wo0[0, D, h0 + lane, :]
        m["wp_init"] = wp
        m["ln_g"] = ln_g.reshape(HD, 1).copy()
        m["ln_b"] = ln_b.reshape(HD, 1).copy()
        m["w1_T"] = np.ascontiguousarray(np.asarray(inputs["L0_ffn_w1_w"], np.float32).T)
        m["w2_T"] = np.ascontiguousarray(np.asarray(inputs["L0_ffn_w2_w"], np.float32).T)
        m["h1_T"] = np.ascontiguousarray(np.asarray(inputs["L0_ffn_h1_w"], np.float32).T)
        m["h2_T"] = np.ascontiguousarray(np.asarray(inputs["L0_ffn_h2_w"], np.float32).T)
        m["w3_T"] = np.ascontiguousarray(np.asarray(inputs["L0_ffn_w3_w"], np.float32).T)
        m["s1"] = np.ascontiguousarray(np.asarray(inputs["L0_ffn_w1_s"], np.float32).reshape(32, 128).T)
        m["s2"] = np.ascontiguousarray(np.asarray(inputs["L0_ffn_w2_s"], np.float32).reshape(32, 128).T)
        m["sh1"] = np.ascontiguousarray(np.asarray(inputs["L0_ffn_h1_s"], np.float32).reshape(32, 128).T)
        m["sh2"] = np.ascontiguousarray(np.asarray(inputs["L0_ffn_h2_s"], np.float32).reshape(32, 128).T)
        m["s3"] = np.ascontiguousarray(np.asarray(inputs["L0_ffn_w3_s"], np.float32).reshape(8, 128).T)
        m.update(ident=ident, halfsel=halfsel, halfbc=halfbc, ones_col=ones_col,
                 ones_row=ones_row, lrsel=lrsel, bd16=bd16, smask=smask,
                 mtril=mtril, smk_l=smk_l, smk_r=smk_r)
        per_core.append(m)
    return per_core



class Streams:
    def __init__(self):
        self.ops = {e: [] for e in ENGINES}


class KB:
    def __init__(self, debug_taps=()):
        self.nc = bass.Bass("TRN2", target_bir_lowering=False)
        self.debug_taps = set(debug_taps)
        self.taps = {}
        self.cnt = {}
        self.st = None
        self._es = None
        self._cur_es = None
        self.t = {}

    def emit(self, eng, meth, *args, waits=(), drain=False, inc=None,
             incv=1, **kw):
        def f(E, S, args=args, kw=kw, waits=tuple(waits)):
            for s, v in waits:
                E.wait_ge(S[s], v)
            if drain:
                E.drain()
            ins = getattr(E, meth)(*args, **kw)
            if inc is not None:
                ins.then_inc(S[inc], incv)
        self.st.ops[eng].append(f)
        if inc is not None:
            self.cnt[inc] += incv
            return self.cnt[inc]
        return None

    def wait(self, eng, sem, val):
        def f(E, S):
            E.wait_ge(S[sem], val)
        self.st.ops[eng].append(f)

    def raw(self, eng, fn):
        self.st.ops[eng].append(fn)

    def inc_here(self, eng, sem, incv=1):
        self.cnt[sem] += incv
        v = self.cnt[sem]
        def f(E, S):
            E.nop().then_inc(S[sem], incv)
        self.st.ops[eng].append(f)
        return v

    def barrier(self):
        tgt = self.cnt["bar"] + 5
        self.cnt["bar"] = tgt
        for e in ENGINES:
            def f(E, S, t=tgt):
                E.nop().then_inc(S["bar"], 1)
                E.wait_ge(S["bar"], t)
            self.st.ops[e].append(f)

    def flush(self, label=None):
        import contextlib
        cm = self.nc.named_scope(label) if label else contextlib.nullcontext()
        with cm:
            for e in ENGINES:
                ops = self.st.ops[e]
                if not ops:
                    continue
                def body(E, ops=ops, S=self.sems):
                    for f in ops:
                        f(E, S)
                getattr(self.block, e)(body)
        self.st = Streams()

    def build(self):
        from contextlib import ExitStack
        nc = self.nc
        self.din = {}
        def din(name, shape, dt=F32):
            self.din[name] = nc.dram_tensor(name, list(shape), dt,
                                            kind="ExternalInput").ap()
        din("hid_T", (DM, L))
        din("hid_half", (DM, TOKH))
        for w in ("wq_T", "wk_T", "wv_T", "wg_T"):
            din(w, (DM, 512))
        din("wo_T", (DM, DM))
        din("wlr_scan", (DM, 64))
        for w in ("cq_s", "ck_s", "cv_s"):
            din(w, (512, 4))
        din("wp_init", (HD, 64))
        din("ln_g", (HD, 1)); din("ln_b", (HD, 1))
        din("w1_T", (DM, 4096)); din("w2_T", (DM, 4096))
        din("h1_T", (4096, 4096)); din("h2_T", (4096, 4096))
        din("w3_T", (4096, DM))
        din("s1", (128, 32)); din("s2", (128, 32)); din("sh1", (128, 32))
        din("sh2", (128, 32)); din("s3", (128, 8))
        din("ident", (128, 128))
        din("halfsel", (128, 2)); din("halfbc", (2, 128))
        din("ones_col", (128, 1)); din("ones_row", (1, 128))
        din("lrsel", (32, 512))
        din("bd16", (128, 16)); din("smask", (64, 128)); din("mtril", (128, 128))
        din("smk_l", (9, 64)); din("smk_r", (9, 128))

        self.out = nc.dram_tensor("out_T", [DM, TOKH], F32, kind="ExternalOutput").ap()

        self.sc_feedA = nc.dram_tensor("sc_feedA", [128, NCH * 128], BF16, kind="Internal")
        self.sc_feedB = nc.dram_tensor("sc_feedB", [128, NCH * 128], BF16, kind="Internal")
        self.sc_gate = nc.dram_tensor("sc_gate", [512, L], BF16, kind="Internal")
        self.sc_o = nc.dram_tensor("sc_o", [HD, NCH * 128], BF16, kind="Internal")
        self.sc_og = nc.dram_tensor("sc_og", [512, L], BF16, kind="Internal")
        self.sc_w1 = nc.dram_tensor("sc_w1", [DM, 4096], BF16, kind="Internal")
        self.sc_w2 = nc.dram_tensor("sc_w2", [DM, 4096], BF16, kind="Internal")
        self.sc_h1 = nc.dram_tensor("sc_h1", [4096, 4096], BF16, kind="Internal")
        self.sc_h2 = nc.dram_tensor("sc_h2", [4096, 4096], BF16, kind="Internal")
        self.sc_w3 = nc.dram_tensor("sc_w3", [4096, DM], BF16, kind="Internal")
        self.og_full = nc.dram_tensor("og_full", [DM, L], BF16, kind="Internal")

        tapspec = {
            "feedA": ([128, NCH * 128], BF16), "feedB": ([128, NCH * 128], BF16),
            "gate": ([512, L], BF16), "lrC": ([HD, L], F32),
            "o": ([HD, NCH * 128], BF16), "og": ([DM, L], BF16),
            "xmid": ([128, 8 * 512], F32),
        }
        for name in self.debug_taps:
            shp, dt = tapspec[name]
            self.taps[name] = nc.dram_tensor("tap_" + name, shp, dt,
                                             kind="ExternalOutput").ap()

        es = ExitStack()
        self._es = es
        with es:
            self.sems = {}
            for s in ("sy", "syB", "gd", "gdB", "gdO", "of", "ofB",
                      "wg0", "wg1", "wg2", "wg3", "wh0", "wh1", "wh2", "wh3",
                      "wcL", "wcS",
                      "pe", "dv", "ac", "gp", "cc", "bar", "so"):
                self.sems[s] = es.enter_context(nc.semaphore("sem_" + s))
                self.cnt[s] = 0
            self.block = es.enter_context(nc.Block())
            self.st = Streams()

            self.pb = [es.enter_context(nc.psum_tensor(f"pb{i}", [128, 512], F32))
                       for i in range(6)]
            self.pbf = [es.enter_context(nc.psum_tensor(f"pbf{i}", [128, 1024], BF16))
                        for i in range(2)]

            self._alloc_resident(es)
            self._p0()
            self.flush('p0')
            self._p1(es)
            self._p2(es)
            self._p3(es)
            self._p4(es)
        return nc

    def SB(self, name, shape, dt):
        es = self._cur_es if self._cur_es is not None else self._es
        h = es.enter_context(self.nc.sbuf_tensor(name, list(shape), dt))
        self.t[name] = h
        return h

    def phase_scope(self):
        from contextlib import ExitStack
        self._cur_es = ExitStack()
        return self._cur_es

    def end_scope(self):
        self._cur_es.close()
        self._cur_es = None

    def _alloc_resident(self, es):
        SB = self.SB
        SB("convq_sb", (128, 4 * 5), F32)
        SB("convk_sb", (128, 4 * 5), F32)
        SB("convv_sb", (128, 4 * 5), F32)
        SB("wp_sb", (HD, 64), F32)
        SB("wpbf_sb", (HD, 64), BF16)
        SB("lng_sb", (HD, 1), F32)
        SB("lnb_sb", (HD, 1), F32)
        SB("ident_sb", (128, 128), F32)
        SB("identb_sb", (128, 128), BF16)
        SB("halfsel_sb", (128, 2), F32)
        SB("halfbc_sb", (2, 128), F32)
        SB("onesc_sb", (128, 1), F32)
        SB("onesr_sb", (1, 128), F32)
        SB("onescb_sb", (128, 1), BF16)
        SB("onesrb_sb", (1, 128), BF16)
        SB("lrsel_sb", (32, 512), F32)
        SB("bd16_sb", (128, 16), F32)
        SB("smask_sb", (64, 128), F32)
        SB("smkl_sb", (9, 64), BF16)
        SB("smkr_sb", (9, 128), BF16)
        SB("mtril_sb", (128, 128), F32)
        SB("cb_sb", (128, 4), F32)
        SB("s1_sb", (128, 32), F32)
        SB("s2_sb", (128, 32), F32)
        SB("sh1_sb", (128, 32), F32)
        SB("sh2_sb", (128, 32), F32)
        SB("s3_sb", (128, 8), F32)

    def _p0(self):
        t, d = self.t, self.din
        def loads(E, S):
            for name, src in (("ident_sb", "ident"), ("halfsel_sb", "halfsel"),
                              ("halfbc_sb", "halfbc"), ("onesc_sb", "ones_col"),
                              ("onesr_sb", "ones_row"), ("lrsel_sb", "lrsel"),
                              ("bd16_sb", "bd16"), ("smask_sb", "smask"),
                              ("mtril_sb", "mtril"), ("wp_sb", "wp_init"),
                              ("lng_sb", "ln_g"), ("lnb_sb", "ln_b"),
                              ("s1_sb", "s1"), ("s2_sb", "s2"),
                              ("sh1_sb", "sh1"), ("sh2_sb", "sh2"),
                              ("s3_sb", "s3")):
                E.dma_start(out=t[name][:], in_=d[src]).then_inc(S["sy"], 16)
        self.st.ops["sync"].append(loads)
        self.cnt["sy"] += 16 * 17
        v_sy = self.cnt["sy"]

        def casts(E, S):
            for name, src in (("convq_sb", "cq_s"), ("convk_sb", "ck_s"),
                              ("convv_sb", "cv_s")):
                E.dma_start(
                    out=t[name][:].rearrange("p (f k) -> p f k", k=5)[:, :, 0:4],
                    in_=d[src].rearrange("(f p) k -> p f k", p=128)
                    ).then_inc(S["gd"], 16)
            E.dma_start(out=t["smkl_sb"][:], in_=d["smk_l"]).then_inc(S["gd"], 16)
            E.dma_start(out=t["smkr_sb"][:], in_=d["smk_r"]).then_inc(S["gd"], 16)
            E.memset(t["cb_sb"][:, 0:1], 1e-5)
            E.memset(t["cb_sb"][:, 1:2], 0.001)
            E.memset(t["cb_sb"][:, 2:3], 1.0)
            ins2 = E.memset(t["cb_sb"][:, 3:4], 0.0)
            ins2.then_inc(S["gp"], 1)
        self.st.ops["gpsimd"].append(casts)
        self.cnt["gp"] += 1
        self.cnt["gd"] += 16 * 5
        v_gd = self.cnt["gd"]

        def vsetup(E, S):
            E.wait_ge(S["sy"], v_sy)
            E.wait_ge(S["gd"], v_gd)
            for name in ("convq_sb", "convk_sb", "convv_sb"):
                cw = t[name]
                for f in range(4):
                    E.tensor_scalar_add(cw[:, f * 5 + 4:f * 5 + 5],
                                        cw[:, f * 5 + 3:f * 5 + 4], 1.0)
            E.tensor_copy(t["identb_sb"][:], t["ident_sb"][:])
            E.tensor_copy(t["onesrb_sb"][:], t["onesr_sb"][:])
            E.tensor_copy(t["onescb_sb"][:], t["onesc_sb"][:])
            ins = E.tensor_copy(t["wpbf_sb"][:], t["wp_sb"][:])
            ins.then_inc(S["dv"], 1)
        self.st.ops["vector"].append(vsetup)
        self.cnt["dv"] += 1
        self.barrier()

    def _p1(self, es):
        t, d, pb = self.t, self.din, self.pb
        from contextlib import ExitStack
        self._es_mixer = ExitStack()
        self._cur_es = self._es_mixer
        SB = self.SB
        SB("lrC_sb", (HD, L), F32)
        self.phase_scope()
        SB("wq_sb", (128, 8 * 512), BF16)
        SB("wk_sb", (128, 8 * 512), BF16)
        SB("wv_sb", (128, 8 * 512), BF16)
        SB("wg_sb", (128, 8 * 512), BF16)
        SB("wls_sb", (128, 8 * 64), BF16)
        em0, wt0 = self.emit, self.wait
        def wcasts(E, S):
            for name, src in (("wq_sb", "wq_T"), ("wk_sb", "wk_T"),
                              ("wv_sb", "wv_T"), ("wg_sb", "wg_T")):
                E.dma_start(out=t[name][:].rearrange("p (a f) -> p a f", a=8),
                            in_=d[src].rearrange("(a p) f -> p a f", p=128)
                            ).then_inc(S["gd"], 16)
            E.dma_start(out=t["wls_sb"][:].rearrange("p (a f) -> p a f", a=8),
                        in_=d["wlr_scan"].rearrange("(a p) f -> p a f", p=128)
                        ).then_inc(S["gd"], 16)
        self.st.ops["gpsimd"].append(wcasts)
        self.cnt["gd"] += 16 * 5
        v_wc = self.cnt["gd"]
        wt0("tensor", "gd", v_wc)
        hidt = SB("p1_hidt", (128, 8 * PT), F32)
        sqt = SB("p1_sqt", (128, 8 * PT), F32)
        ln1t = SB("p1_ln1t", (128, 8 * PT), BF16)
        stat = SB("p1_stat", (1, 4 * PT), F32)
        convh = SB("p1_convh", (128, 2 * (PT + 3)), F32)
        cvt = SB("p1_cvt", (128, 2 * PT), F32)
        tails = SB("p1_tails", (128, 12 * 3), F32)
        silt = SB("p1_silt", (128, 2 * PT), F32)
        sq2t = SB("p1_sq2t", (128, 2 * PT), F32)
        rnorm = SB("p1_rnorm", (2, 4 * PT), F32)
        fint = SB("p1_fint", (128, 2 * PT), F32)

        def _p1_feed(g, kind, f, src):
            dstt, roff = {"k": (asmA, 0), "v": (asmA, 64),
                          "vl": (asmB, 0), "q": (asmB, 64)}[kind]
            last = None
            for hp in range(2):
                lane = 2 * f + hp
                last = self.emit("vector", "tensor_copy",
                                 dstt[roff:roff + 64, :]
                                 .rearrange("p (c l u) -> p c l u", l=8, u=16)
                                 [:, :, lane, :],
                                 src[hp * 64:(hp + 1) * 64, :]
                                 .rearrange("p (c u) -> p c u", u=16),
                                 drain=(hp == 0),
                                 inc="dv" if hp == 1 else None)
            self._p1_done[g] = last
        self._p1_feed = _p1_feed
        asmA = SB("p1_asmA", (128, 32 * 128), BF16)
        asmB = SB("p1_asmB", (128, 32 * 128), BF16)
        gatet = SB("p1_gatet", (128, 4 * PT), BF16)
        lrsp = SB("p1_lrsp", (HD, 4 * PT), F32)

        em, wt = self.emit, self.wait
        em("gpsimd", "memset", tails[:], 0.0, inc="gp")
        v_tails0 = self.cnt["gp"]
        wt("vector", "gp", v_tails0)

        prev_dma = None
        for ti in range(NPT):
            t0 = ti * PT
            v_hid = em("sync", "dma_start", inc="syB", incv=16,
                       out=hidt[:].rearrange("p (a t) -> p a t", a=8),
                       in_=d["hid_T"][:, t0:t0 + PT].rearrange("(a p) t -> p a t", p=128),
                       waits=([("gd", prev_dma)] if prev_dma else []))
            if prev_dma:
                wt("vector", "gd", prev_dma)

            for a in range(8):
                v_sq = em("scalar", "activation",
                          sqt[:, a * PT:(a + 1) * PT], hidt[:, a * PT:(a + 1) * PT],
                          AF.Square,
                          waits=([("syB", v_hid)] if a == 0 else []),
                          inc="ac" if a == 7 else None)
            wt("tensor", "syB", v_hid)
            wt("tensor", "ac", v_sq)
            for a in range(8):
                em("tensor", "matmul", pb[0][0:1, :], t["onesc_sb"][:],
                   hidt[:, a * PT:(a + 1) * PT], start=(a == 0), stop=(a == 7))
            for a in range(8):
                v_sum = em("tensor", "matmul", pb[1][0:1, :], t["onesc_sb"][:],
                           sqt[:, a * PT:(a + 1) * PT], start=(a == 0),
                           stop=(a == 7), inc="pe" if a == 7 else None)
            wt("vector", "pe", v_sum)
            em("vector", "tensor_scalar_mul", stat[0:1, 0:PT], pb[0][0:1, :], 1.0 / DM)
            em("vector", "tensor_mul", stat[0:1, 3 * PT:4 * PT],
               stat[0:1, 0:PT], stat[0:1, 0:PT], drain=True)
            v_var = em("vector", "scalar_tensor_tensor", stat[0:1, PT:2 * PT],
                       pb[1][0:1, :], 1.0 / DM, stat[0:1, 3 * PT:4 * PT],
                       ALU.mult, ALU.subtract, drain=True, inc="dv")
            wt("scalar", "dv", v_var)
            em("scalar", "activation", stat[0:1, 3 * PT:4 * PT],
               stat[0:1, PT:2 * PT], AF.Ln, bias=t["cb_sb"][0:1, 0:1])
            v_A = em("scalar", "activation", stat[0:1, PT:2 * PT],
                     stat[0:1, 3 * PT:4 * PT], AF.Exp, scale=-0.5,
                     drain=True, inc="ac")
            wt("vector", "ac", v_A)
            v_B = em("vector", "scalar_tensor_tensor", stat[0:1, 2 * PT:3 * PT],
                     stat[0:1, 0:PT], -1.0, stat[0:1, PT:2 * PT],
                     ALU.mult, ALU.mult, inc="dv")
            wt("tensor", "dv", v_B)
            em("tensor", "matmul", pb[4][:, :], t["onesr_sb"][:],
               stat[0:1, PT:2 * PT], start=True, stop=True)
            v_bc = em("tensor", "matmul", pb[5][:, :], t["onesr_sb"][:],
                      stat[0:1, 2 * PT:3 * PT], start=True, stop=True, inc="pe")
            wt("vector", "pe", v_bc)
            for a in range(8):
                em("vector", "tensor_mul", sqt[:, a * PT:(a + 1) * PT],
                   hidt[:, a * PT:(a + 1) * PT], pb[4][:, :],
                   drain=(a == 0))
                em("vector", "tensor_add", ln1t[:, a * PT:(a + 1) * PT],
                   sqt[:, a * PT:(a + 1) * PT], pb[5][:, :], drain=True)
            v_ln1 = self.emit("vector", "tensor_copy", ln1t[:, 0:1], ln1t[:, 0:1],
                              drain=True, inc="dv")

            wt("tensor", "dv", v_ln1)
            for a in range(8):
                v_lrmm = em("tensor", "matmul", pb[2][0:64, :],
                            t["wls_sb"][:, a * 64:(a + 1) * 64],
                            ln1t[:, a * PT:(a + 1) * PT],
                            start=(a == 0), stop=(a == 7),
                            inc="pe" if a == 7 else None)
            wt("scalar", "pe", v_lrmm)
            em("scalar", "activation", lrsp[:, 0:PT], pb[2][0:64, :], AF.Abs,
               bias=t["cb_sb"][0:64, 1:2])
            em("scalar", "activation", lrsp[:, PT:2 * PT], lrsp[:, 0:PT],
               AF.Exp, scale=-1.0, drain=True)
            v_sp = em("scalar", "activation", lrsp[:, 2 * PT:3 * PT],
                      lrsp[:, PT:2 * PT], AF.Ln, bias=t["cb_sb"][0:64, 2:3],
                      drain=True, inc="ac")
            wt("vector", "ac", v_sp)
            em("vector", "tensor_scalar", lrsp[:, 3 * PT:4 * PT], pb[2][0:64, :],
               0.001, 0.0, ALU.add, ALU.max)
            v_lrC = em("vector", "tensor_add", t["lrC_sb"][:, t0:t0 + PT],
                       lrsp[:, 3 * PT:4 * PT], lrsp[:, 2 * PT:3 * PT],
                       drain=True, inc="dv")

            projs = (("q", t["wq_sb"], t["convq_sb"], True),
                     ("k", t["wk_sb"], t["convk_sb"], True),
                     ("v", t["wv_sb"], t["convv_sb"], False),
                     ("g", t["wg_sb"], None, False))
            wt("tensor", "dv", v_lrC)
            if ti == 0:
                self._p1_evac = {}
                self._p1_done = {}
                self._p1_ns = {}
            evac = self._p1_evac
            gdone = self._p1_done
            pend = [None]

            def emit_aux(fn):
                if pend[0] is not None:
                    pend[0]()
                pend[0] = fn

            base_g = ti * 16
            for pi, (pname, wsb, csb, do_norm) in enumerate(projs):
                for f in range(4):
                    g = base_g + pi * 4 + f
                    sl = g % 2
                    pp = pb[2 + sl]
                    if g - 2 in evac:
                        wt("tensor", "dv", evac[g - 2])
                    for a in range(8):
                        v_pj = em("tensor", "matmul", pp[:, :],
                                  wsb[:, a * 512 + f * 128: a * 512 + (f + 1) * 128],
                                  ln1t[:, a * PT:(a + 1) * PT],
                                  start=(a == 0), stop=(a == 7),
                                  inc="pe" if a == 7 else None)
                    emit_aux(None) if False else None
                    if pend[0] is not None:
                        pend[0]()
                        pend[0] = None
                    if pname == "g":
                        wt("vector", "pe", v_pj)
                        evac[g] = em("vector", "tensor_copy",
                                     gatet[:, f * PT:(f + 1) * PT], pp[:, :],
                                     inc="dv")
                        continue
                    idx = pi * 4 + f
                    cvh = convh[:, sl * (PT + 3):(sl + 1) * (PT + 3)]
                    cv2 = cvt[:, sl * PT:(sl + 1) * PT]
                    sil = silt[:, sl * PT:(sl + 1) * PT]
                    sq2 = sq2t[:, sl * PT:(sl + 1) * PT]
                    fin = fint[:, sl * PT:(sl + 1) * PT]
                    rn2 = rnorm[:, sl * 2 * PT:(sl + 1) * 2 * PT]
                    wt("vector", "pe", v_pj)
                    if g - 2 in gdone:
                        wt("vector", "dv", gdone[g - 2])
                    em("vector", "tensor_copy", cvh[:, 0:3],
                       tails[:, idx * 3:idx * 3 + 3])
                    evac[g] = em("vector", "tensor_copy", cvh[:, 3:PT + 3],
                                 pp[:, :], inc="dv")
                    em("vector", "tensor_copy", tails[:, idx * 3:idx * 3 + 3],
                       cvh[:, PT:PT + 3], drain=True)
                    em("vector", "tensor_scalar_mul", cv2,
                       cvh[:, 3:PT + 3], csb[:, f * 5 + 4:f * 5 + 5], drain=True)
                    em("vector", "scalar_tensor_tensor", cv2,
                       cvh[:, 2:PT + 2], csb[:, f * 5 + 2:f * 5 + 3], cv2,
                       ALU.mult, ALU.add, drain=True)
                    em("vector", "scalar_tensor_tensor", cv2,
                       cvh[:, 1:PT + 1], csb[:, f * 5 + 1:f * 5 + 2], cv2,
                       ALU.mult, ALU.add, drain=True)
                    v_cv = em("vector", "scalar_tensor_tensor", cv2,
                              cvh[:, 0:PT], csb[:, f * 5 + 0:f * 5 + 1], cv2,
                              ALU.mult, ALU.add, drain=True, inc="dv")
                    wt("scalar", "dv", v_cv)
                    v_sil = em("scalar", "activation", sil, cv2, AF.Silu,
                               inc="ac")
                    if do_norm:
                        if g - 2 in self._p1_ns:
                            wt("scalar", "pe", self._p1_ns[g - 2])
                        v_sq2 = em("scalar", "activation", sq2, sil,
                                   AF.Square, drain=True, inc="ac")

                        def aux_norm(g=g, sl=sl, v_sq2=v_sq2, v_sil=v_sil,
                                     sq2=sq2, sil=sil, fin=fin, rn2=rn2,
                                     pname=pname, f=f):
                            pbank = pb[4] if sl == 0 else pb[5]
                            wt("tensor", "ac", v_sq2)
                            if g - 2 in self._p1_done:
                                wt("tensor", "dv", self._p1_done[g - 2])
                            v_ns = em("tensor", "matmul", pbank[0:2, :],
                                      t["halfsel_sb"][:], sq2,
                                      start=True, stop=True, inc="pe")
                            self._p1_ns[g] = v_ns
                            wt("scalar", "pe", v_ns)
                            em("scalar", "activation", rn2[0:2, 0:PT],
                               pbank[0:2, :], AF.Ln)
                            v_rn = em("scalar", "activation", rn2[0:2, PT:2 * PT],
                                      rn2[0:2, 0:PT], AF.Exp, scale=-0.5,
                                      drain=True, inc="ac")
                            wt("tensor", "ac", v_rn)
                            v_nb = em("tensor", "matmul", pbank[0:128, :],
                                      t["halfbc_sb"][:], rn2[0:2, PT:2 * PT],
                                      start=True, stop=True, inc="pe")
                            wt("vector", "ac", v_sil)
                            wt("vector", "pe", v_nb)
                            em("vector", "tensor_mul", fin, sil, pbank[0:128, :],
                               drain=True)
                            self._p1_feed(g, pname, f, fin)
                        pend[0] = aux_norm
                    else:
                        def aux_v(g=g, sl=sl, v_sil=v_sil, sil=sil, fin=fin,
                                  f=f, t0=t0):
                            pbank = pb[4] if sl == 0 else pb[5]
                            wt("tensor", "ac", v_sil)
                            if g - 2 in self._p1_done:
                                wt("tensor", "dv", self._p1_done[g - 2])
                            v_l2 = em("tensor", "matmul", pbank[0:128, :],
                                      t["lrsel_sb"][:, f * 128:(f + 1) * 128],
                                      t["lrC_sb"][0:32, t0:t0 + PT],
                                      start=True, stop=True, inc="pe")
                            wt("vector", "ac", v_sil)
                            self._p1_feed(g, "v", f, sil)
                            wt("vector", "pe", v_l2)
                            em("vector", "tensor_mul", fin, sil, pbank[0:128, :],
                               drain=True)
                            self._p1_feed(g, "vl", f, fin)
                        pend[0] = aux_v
            if pend[0] is not None:
                pend[0]()
                pend[0] = None

            v_all = self.cnt["dv"]
            wt("gpsimd", "dv", v_all)
            em("gpsimd", "dma_start", out=self.sc_feedA.ap()[:, ti * 4096:(ti + 1) * 4096],
               in_=asmA[:, :], inc="gd", incv=16)
            em("gpsimd", "dma_start", out=self.sc_feedB.ap()[:, ti * 4096:(ti + 1) * 4096],
               in_=asmB[:, :], inc="gd", incv=16)
            for f in range(4):
                prev_dma = em("gpsimd", "dma_start",
                              out=self.sc_gate.ap()[f * 128:(f + 1) * 128, t0:t0 + PT],
                              in_=gatet[:, f * PT:(f + 1) * PT],
                              inc="gd", incv=16)
        if "feedA" in self.taps or "feedB" in self.taps or "gate" in self.taps \
           or "lrC" in self.taps:
            wt("gpsimd", "gd", self.cnt["gd"])
            if "feedA" in self.taps:
                em("gpsimd", "dma_start", out=self.taps["feedA"],
                   in_=self.sc_feedA.ap()[:, :], inc="gd", incv=16)
            if "feedB" in self.taps:
                em("gpsimd", "dma_start", out=self.taps["feedB"],
                   in_=self.sc_feedB.ap()[:, :], inc="gd", incv=16)
            if "gate" in self.taps:
                em("gpsimd", "dma_start", out=self.taps["gate"],
                   in_=self.sc_gate.ap()[:, :], inc="gd", incv=16)
            if "lrC" in self.taps:
                em("gpsimd", "dma_start", out=self.taps["lrC"],
                   in_=t["lrC_sb"][:, :], inc="gd", incv=16)
            wt("gpsimd", "gd", self.cnt["gd"])
        self.barrier()
        self.flush('p1')
        self.end_scope()

    def _p2(self, es):
        t, d, pb, pbf = self.t, self.din, self.pb, self.pbf
        em, wt, SB = self.emit, self.wait, self.SB
        self.phase_scope()

        feedK = SB("p2_feedK", (64, 2 * SW * 128), BF16)
        feedV = SB("p2_feedV", (64, 2 * SW * 128), BF16)
        feedVL = SB("p2_feedVL", (64, 2 * SW * 128), BF16)
        feedQ = SB("p2_feedQ", (64, 2 * SW * 128), BF16)
        o_w = SB("p2_ow", (HD, 2 * OW * 128), F32)
        kvt = SB("p2_kvt", (128, 2 * 128), BF16)
        vlt = SB("p2_vlt", (128, 2 * 64), BF16)
        khqh = SB("p2_khqh", (128, 2 * 64), BF16)
        khqhT = SB("p2_khqhT", (32, 2 * 256), BF16)
        wpD = SB("p2_wpD", (32, 2 * 64), BF16)
        qkmd = SB("p2_qkmd", (128, 2 * 128), BF16)
        ppair = SB("p2_ppair", (HD, 128), BF16)
        pT = SB("p2_pT", (128, 64), BF16)
        qkm = SB("p2_qkm", (128, 128), BF16)
        sxm = SB("p2_sxm", (128, 64), F32)
        sxe = SB("p2_sxe", (128, 64), F32)
        sgm = SB("p2_sgm", (128, 64), F32)
        ssx = SB("p2_ssx", (HD, 128), F32)
        ssx2 = SB("p2_ssx2", (HD, 128), F32)
        sse = SB("p2_sse", (HD, 128), F32)
        ssm = SB("p2_ssm", (HD, 4), F32)

        wp, wpbf = t["wp_sb"], t["wpbf_sb"]
        idb = t["identb_sb"]
        lrC = t["lrC_sb"]
        wcst = SB("p2_wcst", (128, 2 * 8192), BF16)
        wc_jobs = []
        for (src, dst, rows, cols) in (("w1_T", self.sc_w1, DM, 4096),
                                       ("w2_T", self.sc_w2, DM, 4096),
                                       ("h1_T", self.sc_h1, 4096, 4096),
                                       ("h2_T", self.sc_h2, 4096, 4096),
                                       ("w3_T", self.sc_w3, 4096, DM)):
            for ch in range(rows // 256):
                wc_jobs.append((src, dst, ch, cols))
        self._wc_i = 0

        def emit_wcast():
            i = self._wc_i
            if i >= len(wc_jobs):
                return
            self._wc_i += 1
            srcn, dstt, ch, cols = wc_jobs[i]
            sl = i % 2
            stg = wcst[:, sl * 8192:sl * 8192 + 2 * cols]
            em("gpsimd", "dma_start", inc="wcL", incv=16,
               out=stg.rearrange("p (a u) -> p a u", a=2),
               in_=d[srcn][ch * 256:(ch + 1) * 256, :]
               .rearrange("(a p) u -> p a u", p=128),
               waits=([("wcS", (i - 1) * 16)] if i >= 2 else []))
            em("gpsimd", "dma_start", inc="wcS", incv=16,
               out=dstt.ap()[ch * 256:(ch + 1) * 256, :]
               .rearrange("(a p) u -> p a u", p=128),
               in_=stg.rearrange("p (a u) -> p a u", a=2),
               waits=[("wcL", (i + 1) * 16)])

        NW = NCH // SW
        win_sy = [0] * NW
        win_rel = [0, 0]
        o_rel = [0, 0]

        def load_window(w):
            wi = w % 2
            sem = "sy" if wi == 0 else "syB"
            cs = slice(wi * SW * 128, (wi + 1) * SW * 128)
            ds = slice(w * SW * 128, (w + 1) * SW * 128)
            em("sync", "dma_start", inc=sem, incv=16,
               out=feedK[:, cs], in_=self.sc_feedA.ap()[0:64, ds],
               waits=([("pe", win_rel[wi])] if win_rel[wi] else []))
            em("sync", "dma_start", inc=sem, incv=16,
               out=feedV[:, cs], in_=self.sc_feedA.ap()[64:128, ds])
            em("sync", "dma_start", inc=sem, incv=16,
               out=feedVL[:, cs], in_=self.sc_feedB.ap()[0:64, ds])
            win_sy[w] = (sem, em("sync", "dma_start", inc=sem, incv=16,
                                 out=feedQ[:, cs],
                                 in_=self.sc_feedB.ap()[64:128, ds]))

        wt("sync", "gd", self.cnt["gd"])
        load_window(0)
        if NW > 1:
            load_window(1)

        v_state = self.cnt["dv"]
        prev = None
        hist = {}

        def emit_out_mm3(pv):
            wt("tensor", "dv", pv["v_kqt"])
            if pv["v_qkm_prev"]:
                wt("tensor", "dv", pv["v_qkm_prev"])
            pv["v_mm3"] = em("tensor", "matmul", pb[4][0:128, 0:128],
                             khqhT[0:32, pv["sl"] * 256:pv["sl"] * 256 + 128],
                             khqhT[0:32, pv["sl"] * 256 + 128:pv["sl"] * 256 + 256],
                             start=True, stop=True, inc="pe")

        def emit_out_qkm(pv):
            wt("vector", "pe", pv["v_mm3"])
            pv["v_qkm"] = em("vector", "tensor_mul",
                             qkmd[:, pv["sl"] * 128:(pv["sl"] + 1) * 128],
                             pb[4][0:128, 0:128], t["mtril_sb"][:], inc="dv")

        def emit_out_o(pv):
            wt("tensor", "dv", pv["v_qkm"])
            if pv["v_oev_prev"]:
                wt("tensor", "dv", pv["v_oev_prev"])
            if pv["o_rel_wait"]:
                wt("tensor", pv["o_rel_wait"][0], pv["o_rel_wait"][1])
            em("tensor", "matmul", pb[3][0:64, 0:128],
               vlt[:, pv["sl"] * 64:(pv["sl"] + 1) * 64],
               qkmd[:, pv["sl"] * 128:(pv["sl"] + 1) * 128],
               start=True, stop=False)
            pv["v_o"] = em("tensor", "matmul", pb[3][0:64, 0:128],
                           wpD[:, pv["sl"] * 64:(pv["sl"] + 1) * 64],
                           khqhT[0:32, pv["sl"] * 256 + 128:pv["sl"] * 256 + 256],
                           start=False, stop=True, inc="pe")
            hist[pv["c"]]["v_o"] = pv["v_o"]

        def emit_out_evac(pv):
            wt("vector", "pe", pv["v_o"])
            pv["v_oev"] = em("vector", "tensor_copy",
                             o_w[:, pv["ocol"]:pv["ocol"] + 128],
                             pb[3][0:64, 0:128], inc="dv")

        def emit_oflush(pv):
            c1 = pv["c"]
            if (c1 + 1) % OW == 0:
                wo_i1 = (c1 // OW) % 2
                wt("gpsimd", "dv", pv["v_oev"])
                o_rel[wo_i1] = em("gpsimd", "dma_start",
                                  out=self.sc_o.ap()[:, (c1 + 1 - OW) * 128:(c1 + 1) * 128],
                                  in_=o_w[:, wo_i1 * OW * 128:(wo_i1 + 1) * OW * 128],
                                  inc="of" if wo_i1 == 0 else "ofB", incv=16)

        for c in range(NCH):
            w = c // SW
            wi = w % 2
            sl = c % 2
            osl = 1 - sl
            cb = wi * SW * 128 + (c % SW) * 128
            kT = feedK[:, cb:cb + 128]
            vT = feedV[:, cb:cb + 128]
            vlT = feedVL[:, cb:cb + 128]
            qT = feedQ[:, cb:cb + 128]
            wo_i = (c // OW) % 2
            ocol = wo_i * OW * 128 + (c % OW) * 128

            wt("tensor", win_sy[w][0], win_sy[w][1])
            if c >= 2:
                wt("tensor", "dv", hist[c - 2]["rel_kv"])
            em("tensor", "transpose", pbf[0][:, 0:64], kT, idb[0:64, 0:64])
            em("tensor", "transpose", pbf[0][:, 64:128], vT, idb[0:64, 0:64])
            v_t1 = em("tensor", "transpose", pbf[0][:, 128:192],
                      vlT, idb[0:64, 0:64], inc="pe")
            wt("vector", "pe", v_t1)
            if c >= 2 and "v_o" in hist[c - 2]:
                wt("vector", "pe", hist[c - 2]["v_o"])
            em("vector", "tensor_copy", kvt[:, sl * 128:(sl + 1) * 128],
               pbf[0][:, 0:128])
            v_kv = em("vector", "tensor_copy", vlt[:, sl * 64:(sl + 1) * 64],
                      pbf[0][:, 128:192], inc="dv")

            wt("tensor", "dv", v_state)
            em("tensor", "matmul", pb[0][0:128, 0:32], kT, wpbf[:, 0:32],
               start=True, stop=True)
            em("tensor", "matmul", pb[0][0:128, 32:64], qT, wpbf[:, 0:32],
               start=True, stop=True)
            em("tensor", "matmul", pb[1][0:32, 0:128], wpbf[:, 0:32], kT,
               start=True, stop=True)
            v_t5 = em("tensor", "transpose", pbf[1][0:32, 256:320],
                      wpbf[:, 32:64], idb[0:64, 0:64], inc="pe")
            wt("vector", "pe", v_t5)
            v_wpd = em("vector", "tensor_copy", wpD[:, sl * 64:(sl + 1) * 64],
                       pbf[1][0:32, 256:320], inc="dv")

            if prev is not None:
                emit_out_mm3(prev)

            em("vector", "reduce_max",
               sgm[:, 0:16].rearrange("p (g o) -> p g o", o=1),
               pb[0][0:128, 0:64].rearrange("p (g u) -> p g u", u=4),
               axis=AX.X, negate=True, drain=True)
            v_sf1 = em("vector", "tensor_add",
                       sxm[:].rearrange("p (g u) -> p g u", u=4),
                       pb[0][0:128, 0:64].rearrange("p (g u) -> p g u", u=4),
                       sgm[:, 0:16].rearrange("p (g o) -> p g o", o=1)
                       .broadcast_to((128, 16, 4)),
                       drain=True, inc="dv")
            wt("scalar", "dv", v_sf1)
            v_exp1 = em("scalar", "activation", sxe[:], sxm[:], AF.Exp, inc="ac")
            if prev is not None:
                emit_out_qkm(prev)
            wt("vector", "ac", v_exp1)
            em("vector", "reduce_sum",
               sgm[:, 16:32].rearrange("p (g o) -> p g o", o=1),
               sxe[:].rearrange("p (g u) -> p g u", u=4), axis=AX.X, drain=True)
            em("vector", "reciprocal", sgm[:, 32:48], sgm[:, 16:32], drain=True)
            em("vector", "tensor_mul", sgm[:, 48:64], sgm[:, 32:48],
               t["bd16_sb"][:], drain=True)
            v_kh = em("vector", "tensor_mul",
                      khqh[:, sl * 64:(sl + 1) * 64]
                      .rearrange("p (g u) -> p g u", u=4),
                      sxe[:].rearrange("p (g u) -> p g u", u=4),
                      sgm[:, 48:64].rearrange("p (g o) -> p g o", o=1)
                      .broadcast_to((128, 16, 4)), drain=True, inc="dv")

            wt("tensor", "dv", max(v_kh, v_kv))
            v_mm6 = em("tensor", "matmul", pb[2][0:64, 0:32],
                       vlt[:, sl * 64:(sl + 1) * 64],
                       khqh[:, sl * 64:sl * 64 + 32], start=True, stop=True,
                       inc="pe")
            if prev is not None:
                emit_out_o(prev)
            wt("vector", "pe", v_mm6)
            v_wout1 = em("vector", "tensor_add", wpbf[:, 32:64], wp[:, 32:64],
                         pb[2][0:64, 0:32], drain=True, inc="dv")
            em("vector", "tensor_add", wp[:, 32:64], wp[:, 32:64],
               pb[2][0:64, 0:32])

            wt("tensor", "dv", v_kh)
            em("tensor", "transpose", pbf[1][0:32, 0:128],
               khqh[:, sl * 64:sl * 64 + 32], idb[:])
            v_t2 = em("tensor", "transpose", pbf[1][0:32, 128:256],
                      khqh[:, sl * 64 + 32:sl * 64 + 64], idb[:], inc="pe")
            wt("vector", "pe", v_t2)
            v_kqt = em("vector", "tensor_copy",
                       khqhT[:, sl * 256:(sl + 1) * 256],
                       pbf[1][0:32, 0:256], inc="dv")

            for it in range(2):
                if it == 0:
                    wt("tensor", "dv", v_wout1)
                    v_mm9 = em("tensor", "matmul", pb[1][32:64, 0:128],
                               wpbf[:, 32:64], vT, start=True, stop=True,
                               inc="pe")
                else:
                    wt("tensor", "dv", v_st1)
                    em("tensor", "matmul", pb[1][0:32, 0:128], wpbf[:, 0:32],
                       kT, start=True, stop=True)
                    v_mm9 = em("tensor", "matmul", pb[1][32:64, 0:128],
                               wpbf[:, 32:64], vT, start=True, stop=True,
                               inc="pe")
                wt("vector", "pe", v_mm9)
                em("vector", "tensor_add", ssx[:], pb[1][0:64, 0:128],
                   t["smask_sb"][:])
                em("vector", "reduce_max", ssm[:, 0:1], ssx[:], axis=AX.X,
                   negate=True, drain=True)
                v_sp = em("vector", "tensor_scalar_mul", ssm[:, 3:4],
                          ssm[:, 0:1], 0.125, drain=True, inc="dv")
                wt("scalar", "dv", v_sp)
                v_se = em("scalar", "activation", sse[:], ssx[:], AF.Exp,
                          scale=0.125, bias=ssm[:, 3:4],
                          accum_out=ssm[:, 1:2], inc="ac")
                if it == 0 and prev is not None:
                    emit_out_evac(prev)
                    emit_oflush(prev)
                wt("vector", "ac", v_se)
                em("vector", "reciprocal", ssm[:, 2:3], ssm[:, 1:2])
                v_pp = em("vector", "tensor_scalar", ppair[:], sse[:],
                          ssm[:, 2:3], lrC[:, CH * c:CH * c + 1],
                          ALU.mult, ALU.mult, drain=True, inc="dv")
                wt("tensor", "dv", v_pp)
                v_t4 = em("tensor", "transpose", pbf[1][0:128, 320:384],
                          ppair[:], idb[0:64, 0:64], inc="pe")
                wt("vector", "pe", v_t4)
                v_pt = em("vector", "tensor_copy", pT[:], pbf[1][0:128, 320:384],
                          inc="dv")
                wt("tensor", "dv", v_pt)
                em("tensor", "matmul", pb[2][0:64, 64:96],
                   kvt[:, sl * 128 + 64:sl * 128 + 128],
                   pT[:, 0:32], start=True, stop=True)
                v_a = em("tensor", "matmul", pb[2][0:64, 32:64],
                         kvt[:, sl * 128:sl * 128 + 64],
                         pT[:, 32:64], start=True, stop=True, inc="pe")
                wt("vector", "pe", v_a)
                v_st1 = em("vector", "tensor_add", wpbf[:], wp[:],
                           pb[2][0:64, 32:96], drain=True, inc="dv")
                em("vector", "tensor_add", wp[:, 0:64], wp[:, 0:64],
                   pb[2][0:64, 32:96])
            v_state = v_st1

            pv = {"c": c, "sl": sl, "v_kqt": v_kqt, "ocol": ocol,
                  "v_qkm_prev": prev["v_qkm"] if prev else 0,
                  "v_oev_prev": prev["v_oev"] if prev else 0,
                  "o_rel_wait": (("of" if wo_i == 0 else "ofB", o_rel[wo_i])
                                 if o_rel[wo_i] else None)}
            hist[c] = {"rel_kv": v_st1}
            prev = pv

            if c % 3 == 0:
                emit_wcast()
            if (c + 1) % SW == 0:
                win_rel[wi] = self.cnt["pe"]
                if w + 2 < NW:
                    load_window(w + 2)

        emit_out_mm3(prev)
        emit_out_qkm(prev)
        emit_out_o(prev)
        emit_out_evac(prev)
        emit_oflush(prev)
        while self._wc_i < len(wc_jobs):
            emit_wcast()

        if "o" in self.taps:
            wt("gpsimd", "gd", self.cnt["gd"])
            em("gpsimd", "dma_start", out=self.taps["o"],
               in_=self.sc_o.ap()[:, :], inc="gd", incv=16)
        self.barrier()
        self.flush('p2')
        self.end_scope()
        self._es_mixer.close()

    def _p3(self, es):
        import dataclasses
        t, d, pb, pbf = self.t, self.din, self.pb, self.pbf
        em, wt, SB = self.emit, self.wait, self.SB
        self.phase_scope()

        from contextlib import ExitStack
        es_a = ExitStack()
        outer_es = self._cur_es
        self._cur_es = es_a
        o_seg = SB("p3_oseg", (HD, 4096), BF16)
        sqo = SB("p3_sqo", (HD, 2 * 512), F32)
        o2t = SB("p3_o2t", (HD, 2 * 512), F32)
        ostatA = SB("p3_ostatA", (1, 4 * 512), F32)
        ostatB = SB("p3_ostatB", (1, 4 * 512), F32)
        o_n = SB("p3_on", (HD, 2 * 512), BF16)
        gate_w = SB("p3_gatew", (128, 4 * 512), BF16)
        ogbuf = SB("p3_ogbuf", (128, 4 * 512), BF16)
        ogbuf2 = SB("p3_ogbuf2", (128, 4 * 512), BF16)
        onesb64 = SB("p3_ones64b", (HD, 1), BF16)
        em("gpsimd", "memset", onesb64[:], 1.0, inc="gp")
        wt("tensor", "gp", self.cnt["gp"])

        wt("sync", "of", self.cnt["of"])
        wt("sync", "ofB", self.cnt["ofB"])
        prev_seg = 0
        v_gmark = 0
        vg = [0, 0]
        for seg in range(NPT):
            t0 = seg * PT
            v_os = em("sync", "dma_start", inc="sy", incv=16,
                      out=o_seg[:], in_=self.sc_o.ap()[:, seg * 4096:(seg + 1) * 4096],
                      waits=([("gd", prev_seg)] if prev_seg else []))
            for f in range(4):
                v_gw = em("sync", "dma_start", inc="sy", incv=16,
                          out=gate_w[:, f * PT:(f + 1) * PT],
                          in_=self.sc_gate.ap()[f * 128:(f + 1) * 128, t0:t0 + PT])
            wt("scalar", "sy", v_gw)
            wt("tensor", "sy", v_gw)
            for g in range(8):
                pr = g % 2
                osl = o_seg[:, g * 512:(g + 1) * 512]
                sq = sqo[:, pr * 512:(pr + 1) * 512]
                o2 = o2t[:, pr * 512:(pr + 1) * 512]
                on = o_n[:, pr * 512:(pr + 1) * 512]
                os_ = (ostatA if pr == 0 else ostatB)[:]
                pbS = pb[0][32 * pr:32 * pr + 1, :]
                pbQ = pb[1][32 * pr:32 * pr + 1, :]
                pbA = pb[4][64 * pr:64 * pr + 64, :]
                pbB = pb[5][64 * pr:64 * pr + 64, :]
                v_sq = em("scalar", "activation", sq, osl, AF.Square,
                          drain=(g > 0), inc="ac",
                          waits=([("dv", vg[pr])] if vg[pr] else []))
                wt("tensor", "ac", v_sq)
                em("tensor", "matmul", pbS, onesb64[:], osl,
                   start=True, stop=True)
                v_sums = em("tensor", "matmul", pbQ,
                            t["onesc_sb"][0:64, :], sq,
                            start=True, stop=True, inc="pe")
                wt("vector", "pe", v_sums)
                em("vector", "tensor_scalar_mul", os_[:, 0:512],
                   pbS, 1.0 / HD, drain=True)
                em("vector", "tensor_mul", os_[:, 1536:2048],
                   os_[:, 0:512], os_[:, 0:512], drain=True)
                v_var = em("vector", "scalar_tensor_tensor", os_[:, 512:1024],
                           pbQ, 1.0 / HD, os_[:, 1536:2048],
                           ALU.mult, ALU.subtract, drain=True, inc="dv")
                wt("scalar", "dv", v_var)
                em("scalar", "activation", os_[:, 1536:2048],
                   os_[:, 512:1024], AF.Ln, bias=t["cb_sb"][0:1, 0:1],
                   drain=True)
                v_A = em("scalar", "activation", os_[:, 512:1024],
                         os_[:, 1536:2048], AF.Exp, scale=-0.5,
                         drain=True, inc="ac")
                wt("vector", "ac", v_A)
                v_B = em("vector", "scalar_tensor_tensor", os_[:, 1024:1536],
                         os_[:, 0:512], -1.0, os_[:, 512:1024],
                         ALU.mult, ALU.mult, inc="dv")
                wt("tensor", "dv", v_B)
                em("tensor", "matmul", pbA, t["onesr_sb"][:, 0:64],
                   os_[:, 512:1024], start=True, stop=True)
                v_bc = em("tensor", "matmul", pbB,
                          t["onesr_sb"][:, 0:64], os_[:, 1024:1536],
                          start=True, stop=True, inc="pe")
                wt("vector", "pe", v_bc)
                em("vector", "tensor_mul", sq, osl, pbA,
                   drain=True)
                em("vector", "tensor_add", o2, sq, pbB,
                   drain=True)
                em("vector", "tensor_scalar", on, o2, t["lng_sb"][:],
                   t["lnb_sb"][:], ALU.mult, ALU.add, drain=True)
                for f in range(4):
                    for hp in range(2):
                        lane = 2 * f + hp
                        v_gmark = em("vector", "tensor_copy",
                           ogbuf[64 * hp:64 * hp + 64, f * 512 + g * 64:
                                 f * 512 + (g + 1) * 64],
                           on.rearrange("p (cc l u) -> p cc l u", l=8, u=16)
                           [:, :, lane, :], drain=(f == 0 and hp == 0),
                           inc="dv" if (f, hp) == (3, 1) else None)
                vg[pr] = v_gmark
            if prev_seg:
                wt("vector", "gd", prev_seg)
            v_og = em("vector", "tensor_mul", ogbuf2[:], ogbuf[:], gate_w[:],
                      drain=True, inc="dv")
            wt("gpsimd", "dv", v_og)
            for f in range(4):
                prev_seg = em("gpsimd", "dma_start",
                              out=self.sc_og.ap()[f * 128:(f + 1) * 128, t0:t0 + PT],
                              in_=ogbuf2[:, f * PT:(f + 1) * PT],
                              inc="gd", incv=16)

        wt("gpsimd", "gd", self.cnt["gd"])
        v_cc = em("gpsimd", "collective_compute", "AllGather", ALU.bypass,
                  replica_groups=[[0, 1], [2, 3], [4, 5], [6, 7]],
                  ins=[self.sc_og.ap().opt()], outs=[self.og_full.ap().opt()],
                  inc="cc")
        if "og" in self.taps:
            wt("gpsimd", "cc", v_cc)
            em("gpsimd", "dma_start", out=self.taps["og"],
               in_=self.og_full.ap()[:, :], inc="gd", incv=16)

        es_a.close()
        self._cur_es = outer_es
        wo_sb = SB("p3_wo", (128, 8 * 1024), BF16)
        ogf = SB("p3_ogf", (128, 8 * 512), BF16)
        hidh = SB("p3_hidh", (128, 8 * 512), F32)
        xmid = SB("p3_xmid", (128, 8 * 512), F32)
        sq8 = hidh
        ln2t = SB("p3_ln2", (128, 8 * 512), BF16)
        lstat = SB("p3_lstat", (1, 4 * 512), F32)
        hbuf = SB("p3_h", (128, 32 * 512), BF16)
        hbuf2 = SB("p3_h2", (128, 32 * 512), BF16)
        w12s = SB("p3_w12s", (128, 4 * 2 * 8 * 128), BF16)
        whs = SB("p3_whs", (128, 4 * 32 * 128), BF16)
        ht1 = SB("p3_ht1", (128, 2 * 512), BF16)
        ht2 = SB("p3_ht2", (128, 2 * 512), F32)
        obuf = SB("p3_obuf", (128, 2 * 512), F32)

        v_woc = em("gpsimd", "dma_start", inc="gdO", incv=16,
                   out=wo_sb[:].rearrange("p (a r) -> p a r", a=8),
                   in_=d["wo_T"].rearrange("(a p) r -> p a r", p=128))

        def dyn_og_load(tl0, dst3, wv):
            def f(E, S, tl0=tl0, wv=wv):
                for sem, val in wv:
                    E.wait_ge(S[sem], val)
                if getattr(self, "_paroff", None) is None:
                    pid = E.partition_id()
                    self._paroff = E.compute_val((pid % 2) * TOKH)
                base = self.og_full.ap().rearrange("(a p) u -> p a u",
                                                   p=128)[:, :, tl0:tl0 + 512]
                dynap = dataclasses.replace(base,
                                            offset=base.offset + self._paroff)
                E.dma_start(out=dst3, in_=dynap).then_inc(S["gdO"], 16)
            self.st.ops["gpsimd"].append(f)
            self.cnt["gdO"] += 16
            return self.cnt["gdO"]

        v_ffn_rel = 0
        v_wrel = [0, 0]
        for tb in range(TOKH // FFB):
            tl0 = tb * FFB
            wvl = [("cc", v_cc)] + ([("dv", v_ffn_rel)] if v_ffn_rel else [])
            v_ogf = dyn_og_load(tl0, ogf[:].rearrange("p (a u) -> p a u", a=8),
                                wvl)
            v_hid = em("sync", "dma_start", inc="sy", incv=16,
                       out=hidh[:].rearrange("p (a u) -> p a u", a=8),
                       in_=d["hid_half"][:, tl0:tl0 + 512]
                       .rearrange("(a p) u -> p a u", p=128),
                       waits=([("dv", v_ffn_rel)] if v_ffn_rel
                              else [("cc", v_cc)]))
            wt("tensor", "gdO", max(v_ogf, v_woc))
            wt("tensor", "sy", v_hid)
            for m in range(8):
                pp = pb[2 + m % 2]
                if m >= 2:
                    wt("tensor", "dv", self.cnt["dv"])
                for a in range(8):
                    v_wo = em("tensor", "matmul", pp[:, :],
                              wo_sb[:, a * 1024 + m * 128:a * 1024 + (m + 1) * 128],
                              ogf[:, a * 512:(a + 1) * 512],
                              start=(a == 0), stop=(a == 7),
                              inc="pe" if a == 7 else None)
                wt("vector", "pe", v_wo)
                em("vector", "tensor_add", xmid[:, m * 512:(m + 1) * 512],
                   hidh[:, m * 512:(m + 1) * 512], pp[:, :], inc="dv")
            v_lnw = self.cnt["dv"]
            for a in range(8):
                v_sq = em("scalar", "activation", sq8[:, a * 512:(a + 1) * 512],
                          xmid[:, a * 512:(a + 1) * 512], AF.Square,
                          waits=([("dv", v_lnw)] if a == 0 else []),
                          inc="ac" if a == 7 else None)
            wt("tensor", "dv", v_lnw)
            wt("tensor", "ac", v_sq)
            for a in range(8):
                em("tensor", "matmul", pb[0][0:1, :], t["onesc_sb"][:],
                   xmid[:, a * 512:(a + 1) * 512], start=(a == 0), stop=(a == 7))
            for a in range(8):
                v_sums = em("tensor", "matmul", pb[1][0:1, :], t["onesc_sb"][:],
                            sq8[:, a * 512:(a + 1) * 512], start=(a == 0),
                            stop=(a == 7), inc="pe" if a == 7 else None)
            wt("vector", "pe", v_sums)
            em("vector", "tensor_scalar_mul", lstat[0:1, 0:512], pb[0][0:1, :],
               1.0 / DM, drain=True)
            em("vector", "tensor_mul", lstat[0:1, 1536:2048], lstat[0:1, 0:512],
               lstat[0:1, 0:512], drain=True)
            v_var = em("vector", "scalar_tensor_tensor", lstat[0:1, 512:1024],
                       pb[1][0:1, :], 1.0 / DM, lstat[0:1, 1536:2048],
                       ALU.mult, ALU.subtract, drain=True, inc="dv")
            wt("scalar", "dv", v_var)
            em("scalar", "activation", lstat[0:1, 1536:2048],
               lstat[0:1, 512:1024], AF.Ln, bias=t["cb_sb"][0:1, 0:1], drain=True)
            v_A = em("scalar", "activation", lstat[0:1, 512:1024],
                     lstat[0:1, 1536:2048], AF.Exp, scale=-0.5, drain=True,
                     inc="ac")
            wt("vector", "ac", v_A)
            v_B = em("vector", "scalar_tensor_tensor", lstat[0:1, 1024:1536],
                     lstat[0:1, 0:512], -1.0, lstat[0:1, 512:1024],
                     ALU.mult, ALU.mult, inc="dv")
            wt("tensor", "dv", v_B)
            em("tensor", "matmul", pb[4][:, :], t["onesr_sb"][:],
               lstat[0:1, 512:1024], start=True, stop=True)
            v_bc = em("tensor", "matmul", pb[5][:, :], t["onesr_sb"][:],
                      lstat[0:1, 1024:1536], start=True, stop=True, inc="pe")
            wt("vector", "pe", v_bc)
            for a in range(8):
                em("vector", "tensor_mul", sq8[:, a * 512:(a + 1) * 512],
                   xmid[:, a * 512:(a + 1) * 512], pb[4][:, :], drain=(a == 0))
                em("vector", "tensor_add", ln2t[:, a * 512:(a + 1) * 512],
                   sq8[:, a * 512:(a + 1) * 512], pb[5][:, :], drain=True)
            v_ln2 = self.emit("vector", "tensor_copy", ln2t[:, 0:1],
                              ln2t[:, 0:1], drain=True, inc="dv")

            WSEM = ("wg0", "wg1", "wg2", "wg3")
            WHSEM = ("wh0", "wh1", "wh2", "wh3")
            if tb == 0:
                wt("sync", "wcS", self.cnt["wcS"])
            if not hasattr(self, "_wrel"):
                self._wrel = {}
            wrel = self._wrel

            WS = {"w1_T": self.sc_w1, "w2_T": self.sc_w2, "h1_T": self.sc_h1,
                  "h2_T": self.sc_h2, "w3_T": self.sc_w3}

            def w12_dma(fo):
                sl = fo % 4
                woff = sl * 16 * 128
                wv = [("pe", wrel[sl])] if wrel.get(sl) else []
                em("sync", "dma_start", inc=WSEM[sl], incv=16,
                   out=w12s[:, woff:woff + 8 * 128]
                   .rearrange("p (a u) -> p a u", a=8),
                   in_=WS["w1_T"].ap()[:, fo * 128:(fo + 1) * 128]
                   .rearrange("(a p) u -> p a u", p=128), waits=wv)
                return em("sync", "dma_start", inc=WSEM[sl], incv=16,
                          out=w12s[:, woff + 8 * 128:woff + 16 * 128]
                          .rearrange("p (a u) -> p a u", a=8),
                          in_=WS["w2_T"].ap()[:, fo * 128:(fo + 1) * 128]
                          .rearrange("(a p) u -> p a u", p=128))

            def wh_dma(wname, fo):
                eng, sems = (("gpsimd", WHSEM) if wname in ("h1_T", "w3_T")
                             else ("sync", WSEM))
                sl = fo % 4
                woff = sl * 32 * 128
                wv = [("pe", wrel[sl])] if wrel.get(sl) else []
                return em(eng, "dma_start", inc=sems[sl], incv=16,
                          out=whs[:, woff:woff + 32 * 128]
                          .rearrange("p (a u) -> p a u", a=32),
                          in_=WS[wname].ap()[:, fo * 128:(fo + 1) * 128]
                          .rearrange("(a p) u -> p a u", p=128), waits=wv)

            vdma = {}
            for fo in range(4):
                vdma[fo] = w12_dma(fo)
            v_sil = {}
            v_h = {}
            for fo in range(32):
                sl = fo % 4
                woff = sl * 16 * 128
                pA = pb[2 + 2 * (fo % 2)]
                pB = pb[3 + 2 * (fo % 2)]
                wt("tensor", WSEM[sl], vdma[fo])
                if fo == 0:
                    wt("tensor", "dv", v_ln2)
                    wt("tensor", "ac", self.cnt["ac"])
                if fo >= 2:
                    wt("tensor", "ac", v_sil[fo - 2])
                    wt("tensor", "dv", v_h[fo - 2])
                for a in range(8):
                    em("tensor", "matmul", pA[:, :],
                       w12s[:, woff + a * 128:woff + (a + 1) * 128],
                       ln2t[:, a * 512:(a + 1) * 512],
                       start=(a == 0), stop=(a == 7))
                for a in range(8):
                    v_s1m = em("tensor", "matmul", pB[:, :],
                               w12s[:, woff + (8 + a) * 128:woff + (9 + a) * 128],
                               ln2t[:, a * 512:(a + 1) * 512],
                               start=(a == 0), stop=(a == 7),
                               inc="pe" if a == 7 else None)
                wrel[sl] = v_s1m
                if fo + 4 < 32:
                    vdma[fo + 4] = w12_dma(fo + 4)
                hsl = fo % 2
                wt("scalar", "pe", v_s1m)
                if fo >= 2:
                    wt("scalar", "dv", v_h[fo - 2])
                v_sil[fo] = em("scalar", "activation", ht1[:, hsl * 512:(hsl + 1) * 512],
                               pA[:, :], AF.Silu,
                               scale=t["s1_sb"][:, fo:fo + 1], inc="ac")
                wt("vector", "ac", v_sil[fo])
                em("vector", "tensor_scalar_mul", ht2[:, hsl * 512:(hsl + 1) * 512],
                   pB[:, :], t["s2_sb"][:, fo:fo + 1])
                v_h[fo] = em("vector", "tensor_mul",
                             hbuf[:, fo * 512:(fo + 1) * 512],
                             ht1[:, hsl * 512:(hsl + 1) * 512],
                             ht2[:, hsl * 512:(hsl + 1) * 512],
                             drain=True, inc="dv")

            for stage, (wsrc, ssrc, src, dst) in enumerate((
                    ("h1_T", "sh1_sb", hbuf, hbuf2),
                    ("h2_T", "sh2_sb", hbuf2, hbuf))):
                v_in = self.cnt["dv"] if stage == 0 else None
                v_in_ac = self.cnt["ac"]
                ws = WHSEM if wsrc in ("h1_T", "w3_T") else WSEM
                vdma = {}
                for fo in range(4):
                    vdma[fo] = wh_dma(wsrc, fo)
                v_g = {}
                for fo in range(32):
                    sl = fo % 4
                    woff = sl * 32 * 128
                    pp = pb[2 + sl]
                    wt("tensor", ws[sl], vdma[fo])
                    if fo == 0:
                        if v_in is not None:
                            wt("tensor", "dv", v_in)
                        else:
                            wt("tensor", "ac", v_in_ac)
                    if fo >= 4:
                        wt("tensor", "ac", v_g[fo - 4])
                    for a in range(32):
                        v_hm = em("tensor", "matmul", pp[:, :],
                                  whs[:, woff + a * 128:woff + (a + 1) * 128],
                                  src[:, a * 512:(a + 1) * 512],
                                  start=(a == 0), stop=(a == 31),
                                  inc="pe" if a == 31 else None)
                    wrel[sl] = v_hm
                    if fo + 4 < 32:
                        vdma[fo + 4] = wh_dma(wsrc, fo + 4)
                    wt("scalar", "pe", v_hm)
                    v_g[fo] = em("scalar", "activation",
                                 dst[:, fo * 512:(fo + 1) * 512], pp[:, :],
                                 AF.Gelu, scale=t[ssrc][:, fo:fo + 1],
                                 drain=True, inc="ac")

            v_h3 = self.cnt["ac"]
            wt("tensor", "ac", v_h3)
            v_ob = {}
            vdma = {}
            for m in range(4):
                vdma[m] = wh_dma("w3_T", m)
            for m in range(8):
                sl = m % 4
                woff = sl * 32 * 128
                pp = pb[2 + sl]
                wt("tensor", WHSEM[sl], vdma[m])
                if m >= 4:
                    wt("tensor", "dv", v_ob[m - 4])
                for a in range(32):
                    v_o4 = em("tensor", "matmul", pp[:, :],
                              whs[:, woff + a * 128:woff + (a + 1) * 128],
                              hbuf[:, a * 512:(a + 1) * 512],
                              start=(a == 0), stop=(a == 31),
                              inc="pe" if a == 31 else None)
                wrel[sl] = v_o4
                if m + 4 < 8:
                    vdma[m + 4] = wh_dma("w3_T", m + 4)
                wt("vector", "pe", v_o4)
                if m >= 2:
                    wt("vector", "so", v_so[m - 1])
                v_ob[m] = em("vector", "scalar_tensor_tensor",
                             obuf[:, (m % 2) * 512:(m % 2 + 1) * 512], pp[:, :],
                             t["s3_sb"][:, m:m + 1], xmid[:, m * 512:(m + 1) * 512],
                             ALU.mult, ALU.add, drain=True, inc="dv")
                wt("gpsimd", "dv", v_ob[m])
                if m == 0:
                    v_so = {}
                v_so[m] = em("gpsimd", "dma_start",
                             out=self.out[m * 128:(m + 1) * 128, tl0:tl0 + 512],
                             in_=obuf[:, (m % 2) * 512:(m % 2 + 1) * 512],
                             inc="so", incv=16)

            v_ffn_rel = self.cnt["dv"]
        if "xmid" in self.taps:
            wt("gpsimd", "dv", self.cnt["dv"])
            em("gpsimd", "dma_start", out=self.taps["xmid"],
               in_=xmid[:, :], inc="gd", incv=16)
        wt("gpsimd", "so", self.cnt["so"])
        self.barrier()
        self.flush('p3')
        self.end_scope()

    def _p4(self, es):
        pass


def build_kernel(debug_taps=()):
    kb = KB(debug_taps)
    nc = kb.build()
    return nc, kb


def kernel(**inputs):
    per_core = host_prep(inputs)
    nc, kb = build_kernel()
    res = run_bass_kernel_spmd(nc, per_core, core_ids=list(range(8)))
    full = np.zeros((4, L, DM), np.float32)
    for b in range(4):
        full[b, 0:TOKH, :] = np.asarray(res.results[2 * b]["out_T"]).T
        full[b, TOKH:L, :] = np.asarray(res.results[2 * b + 1]["out_T"]).T
    return full
